# version 02
# baseline (speedup 1.0000x reference)
import math
import numpy as np
import ml_dtypes
from contextlib import ExitStack

import concourse.bass as bass
import concourse.mybir as mybir
from concourse.bass_utils import run_bass_kernel_spmd

F32 = mybir.dt.float32
F16 = mybir.dt.float16
BF16 = mybir.dt.bfloat16
U8 = mybir.dt.uint8
U16 = mybir.dt.uint16
AF = mybir.ActivationFunctionType
ALU = mybir.AluOpType

D = 2048
Bb = 2
S = 16384
TALL = Bb * S
TCORE = 4096
KC = 16
IFF = 5632
GU_OUT = 11264
NT2 = 256
NTILES2 = TALL // NT2
CH = 128
NCHUNK = S // CH
NGRP = 4
NGROUPS = NCHUNK // NGRP
LNSC = 128.0 ** -0.5
LN_LNSC = math.log(LNSC)
NEGBIG = -1e30
NT3 = 512
NTILES3 = TCORE // NT3


def _interleave_perm(nfull):
    p = []
    for c in range(nfull // 256):
        b = 256 * c
        p.extend(b + 2 * np.arange(128))
        p.extend(b + 2 * np.arange(128) + 1)
    return np.asarray(p, dtype=np.int64)


SIGMA = _interleave_perm(D)
PI = _interleave_perm(IFF)
PI_GU = np.concatenate([PI, IFF + PI])


def build_nc(debug_scratch=False):
    nc = bass.Bass("TRN2", target_bir_lowering=False)
    SK = "ExternalOutput" if debug_scratch else "Internal"

    xT = nc.dram_tensor("xT", [D, TALL], F32, kind="ExternalInput")
    xTo = nc.dram_tensor("xTo", [KC, 128, TCORE], F32, kind="ExternalInput")
    wqkvzT = nc.dram_tensor("wqkvzT", [KC, 128, 768], F32, kind="ExternalInput")
    wbaT = nc.dram_tensor("wbaT", [KC, 128, 34], F32, kind="ExternalInput")
    convw = nc.dram_tensor("convw", [4, 128, 4], F32, kind="ExternalInput")
    gdt = nc.dram_tensor("gdt", [34, 2], F32, kind="ExternalInput")
    normw = nc.dram_tensor("normw", [128, 1], F32, kind="ExternalInput")
    woutT = nc.dram_tensor("woutT", [KC, KC, 128, 128], F32, kind="ExternalInput")
    gub = nc.dram_tensor("gub", [8, 22, 128, 512], U8, kind="ExternalInput")
    gus = nc.dram_tensor("gus", [16, GU_OUT], F16, kind="ExternalInput")
    dnb = nc.dram_tensor("dnb", [22, 4, 128, 512], U8, kind="ExternalInput")
    dns = nc.dram_tensor("dns", [44, D], F16, kind="ExternalInput")
    consts = nc.dram_tensor("consts", [128, 800], F32, kind="ExternalInput")

    out_d = nc.dram_tensor("out", [KC, 128, TCORE], F32, kind="ExternalOutput")

    sc_gu = nc.dram_tensor("sc_gu", [GU_OUT // 128, KC, 128, 128], F32, kind="Internal")
    sc_dn = nc.dram_tensor("sc_dn", [4, 44, 128, 512], F32, kind="Internal")
    sc_q = nc.dram_tensor("sc_q", [128, TALL], F32, kind=SK)
    sc_k = nc.dram_tensor("sc_k", [128, TALL], F32, kind=SK)
    sc_v = nc.dram_tensor("sc_v", [2, 128, TALL], F32, kind=SK)
    sc_z = nc.dram_tensor("sc_z", [2, 128, TALL], F32, kind=SK)
    sc_gb = nc.dram_tensor("sc_gb", [4, TALL], F32, kind=SK)
    sc_rows = nc.dram_tensor("sc_rows", [2, TALL], F32, kind=SK)
    sc_h = nc.dram_tensor("sc_h", [2, 44, 128, NT3], F32, kind="Internal")
    sc_y = nc.dram_tensor("sc_y", [KC, 128, TCORE], F32, kind=SK)
    ag_in = nc.dram_tensor("ag_in", [256, TALL], F32, kind="Internal")
    if debug_scratch:
        dbg_ag = nc.dram_tensor("dbg_ag", [256, TALL], F32, kind="ExternalOutput")
    ag_out = nc.dram_tensor("ag_out", [D, TALL], F32, kind="Internal")
    ag_own = nc.dram_tensor("ag_own", [KC, 128, TCORE], F32, kind="Internal")
    rows_own = nc.dram_tensor("rows_own", [1, 2, TCORE], F32, kind="Internal")

    es = ExitStack()
    n = {}
    EV = {}

    def cnt(s):
        return n.get(s, 0)

    with es:
        sems = {nm: es.enter_context(nc.semaphore(nm)) for nm in
                ["ld", "pe", "dv", "sc", "gp", "ag"]}

        def inc(instr, s, k=1):
            instr.then_inc(sems[s], k)
            n[s] = n.get(s, 0) + k
            return n[s]

        blk = es.enter_context(nc.Block())

        cons = es.enter_context(nc.sbuf_tensor("cons", [128, 800], F32))
        gdt_t = es.enter_context(nc.sbuf_tensor("gdt_t", [34, 2], F32))
        negea_t = es.enter_context(nc.sbuf_tensor("negea_t", [34, 1], F32))
        normw_t = es.enter_context(nc.sbuf_tensor("normw_t", [128, 1], F32))

        for col, val in [(769, 1e-5), (770, 1e-6), (771, 1.0), (772, LN_LNSC),
                         (773, 1.0 / 128.0), (774, 1.0 / D)]:
            nc.const_aps.register(F32, val, cons.ap()[:, col:col + 1])
        ident = cons.ap()[:, 0:128]
        maskS = cons.ap()[:, 128:256]
        maskI = cons.ap()[:, 256:384]
        onescol = cons.ap()[:, 384:385]
        ones1 = cons.ap()[0:1, 384:385]
        onesrow = cons.ap()[0:1, 512:640]
        selrow = cons.ap()[0:2, 640:768]

        p2es = ExitStack()
        wqkv_t = p2es.enter_context(nc.sbuf_tensor("wqkv_t", [128, KC, 768], F32))
        wba_t = p2es.enter_context(nc.sbuf_tensor("wba_t", [128, KC, 34], F32))
        wsum_t = p2es.enter_context(nc.sbuf_tensor("wsum_t", [1, 802], F32))
        convw_t = p2es.enter_context(nc.sbuf_tensor("convw_t", [128, 4, 4], F32))
        ps_dq = p2es.enter_context(nc.psum_tensor("ps_dq", [128, 512], F32))
        ps_w0 = ps_dq

        @blk.sync
        def _(sync):
            for dst, src in [(cons[:, :], consts[:, :]), (gdt_t[:, :], gdt[:, :]),
                             (normw_t[:, :], normw[:, :])]:
                inc(sync.dma_start(dst, src), "ld", 16)
            inc(sync.dma_start(wqkv_t.ap(), wqkvzT.ap().rearrange("kc p m -> p kc m")), "ld", 16)
            inc(sync.dma_start(wba_t.ap(), wbaT.ap().rearrange("kc p m -> p kc m")), "ld", 16)
            inc(sync.dma_start(convw_t.ap(), convw.ap().rearrange("c p j -> p c j")), "ld", 16)
        LD0 = cnt("ld")

        @blk.tensor
        def _(tensor):
            tensor.wait_ge(sems["ld"], LD0)
            for kc in range(KC):
                tensor.matmul(ps_w0[0:1, 0:384], onescol, wqkv_t[:, kc, 0:384],
                              start=(kc == 0), stop=(kc == KC - 1))
            for kc in range(KC):
                tensor.matmul(ps_w0[32:33, 0:384], onescol, wqkv_t[:, kc, 384:768],
                              start=(kc == 0), stop=(kc == KC - 1), tile_position=(0, 32))
            for kc in range(KC):
                m = tensor.matmul(ps_w0[64:65, 0:34], onescol, wba_t[:, kc, :],
                                  start=(kc == 0), stop=(kc == KC - 1), tile_position=(0, 64))
            inc(m, "pe")
        PE0 = cnt("pe")

        @blk.scalar
        def _(scalar):
            scalar.wait_ge(sems["ld"], LD0)
            inc(scalar.activation(negea_t[32:34, 0:1], gdt_t[32:34, 1:2], AF.Exp), "sc")
        SC0 = cnt("sc")

        @blk.vector
        def _(vector):
            vector.wait_ge(sems["pe"], PE0)
            vector.tensor_copy(wsum_t[0:1, 0:384], ps_w0[0:1, 0:384])
            vector.tensor_copy(wsum_t[0:1, 384:768], ps_w0[32:33, 0:384])
            vector.tensor_copy(wsum_t[0:1, 768:802], ps_w0[64:65, 0:34])
            vector.wait_ge(sems["sc"], SC0)
            inc(vector.tensor_scalar_mul(negea_t[32:34, 0:1], negea_t[32:34, 0:1], -1.0), "dv")
        DV0 = cnt("dv")

        deq_slices = [("gu", c, os) for c in range(8) for os in range(22)] + \
                     [("dn", c, os) for c in range(22) for os in range(4)]
        NDQ = len(deq_slices)

        dqb = [p2es.enter_context(nc.sbuf_tensor(f"dqb{i}", [128, 512], U8)) for i in range(2)]
        dq16 = p2es.enter_context(nc.sbuf_tensor("dq16", [128, 512], U16))
        dqn = p2es.enter_context(nc.sbuf_tensor("dqn", [128, 512], U16))
        dqm = p2es.enter_context(nc.sbuf_tensor("dqm", [128, 512], U16))
        dqf = p2es.enter_context(nc.sbuf_tensor("dqf", [128, 512], U16))
        dqg = dqn
        dqw = [p2es.enter_context(nc.sbuf_tensor("dqw0", [128, 2, 512], F32))] * 2
        sgrow = [p2es.enter_context(nc.sbuf_tensor(f"sgrow{i}", [2, 512], F32)) for i in range(2)]

        xt_b = [p2es.enter_context(nc.sbuf_tensor(f"xt{i}", [128, KC, NT2], F32)) for i in range(3)]
        sq_b = p2es.enter_context(nc.sbuf_tensor("sqb", [128, KC, NT2], F32))
        xr_b = [p2es.enter_context(nc.sbuf_tensor(f"xr{i}", [128, KC, NT2], F32)) for i in range(2)]
        rrow_b = [p2es.enter_context(nc.sbuf_tensor(f"rrow{i}", [1, 2, NT2], F32)) for i in range(3)]
        ext_b = [p2es.enter_context(nc.sbuf_tensor(f"ext{i}", [128, 4, NT2 + 3], F32)) for i in range(2)]
        csil = [p2es.enter_context(nc.sbuf_tensor(f"csil{i}", [128, 4, NT2], F32)) for i in range(3)]
        zt_b = [p2es.enter_context(nc.sbuf_tensor(f"zt{i}", [128, 2, NT2], F32)) for i in range(2)]
        gbt_b = [p2es.enter_context(nc.sbuf_tensor(f"gbt{i}", [34, NT2], F32)) for i in range(3)]
        sql2 = [p2es.enter_context(nc.sbuf_tensor(f"sql2_{i}", [128, 2, NT2], F32)) for i in range(2)]
        l2row = p2es.enter_context(nc.sbuf_tensor("l2row", [1, 4, NT2], F32))
        tmpr = p2es.enter_context(nc.sbuf_tensor("tmpr", [1, NT2], F32))
        ps_qkv = p2es.enter_context(nc.psum_tensor("ps_qkv", [128, 6, NT2], F32))
        ps_ba = p2es.enter_context(nc.psum_tensor("ps_ba", [128, NT2], F32))
        ps_st2 = p2es.enter_context(nc.psum_tensor("ps_st2", [128, NT2], F32))
        ps_bc = p2es.enter_context(nc.psum_tensor("ps_bc", [128, NT2], F32))
        ps_l2 = p2es.enter_context(nc.psum_tensor("ps_l2", [128, 2, NT2], F32))

        def live(t):
            return 0 <= t < NTILES2

        for e in range(NTILES2 + 6):
            dq_ids = [i for i in (2 * e - 2, 2 * e - 1) if 0 <= i < NDQ]

            @blk.sync
            def _(sync, e=e, dq_ids=dq_ids):
                t = e
                if live(t):
                    if t >= 3:
                        sync.wait_ge(sems["dv"], EV[("xr", t - 3)])
                    for i in range(4):
                        m = sync.dma_start(
                            xt_b[t % 3][:, 4 * i:4 * i + 4, :],
                            xT.ap().rearrange("(kc p) tt -> p kc tt", p=128)
                              [:, 4 * i:4 * i + 4, t * NT2:(t + 1) * NT2])
                        inc(m, "ld", 16)
                    EV[("xld", t)] = cnt("ld")
                for i in dq_ids:
                    kind, c, os = deq_slices[i]
                    if i >= 2:
                        sync.wait_ge(sems["gp"], EV[("dqst", i - 2)])
                    src = gub[c, os] if kind == "gu" else dnb[c, os]
                    inc(sync.dma_start(dqb[i % 2][:, :], src[:, :]), "ld", 16)
                    EV[("dqld", i)] = cnt("ld")

            @blk.scalar
            def _(scalar, e=e):
                t = e - 1
                if live(t):
                    scalar.wait_ge(sems["ld"], EV[("xld", t)])
                    if t >= 1:
                        scalar.wait_ge(sems["pe"], EV[("stats", t - 1)])
                    for kc in range(KC):
                        m = scalar.activation(sq_b[:, kc, :],
                                              xt_b[t % 3][:, kc, :], AF.Square)
                    EV[("sq", t)] = inc(m, "sc")

            @blk.tensor
            def _(tensor, e=e):
                t = e - 1
                if live(t):
                    tensor.wait_ge(sems["ld"], EV[("xld", t)])
                    if t >= 1:
                        tensor.wait_ge(sems["dv"], EV[("rowsv", t - 1)])
                    tensor.wait_ge(sems["sc"], EV[("sq", t)])
                    for kc in range(KC):
                        tensor.matmul(ps_st2[0:1, :], onescol, xt_b[t % 3][:, kc, :],
                                      start=(kc == 0), stop=(kc == KC - 1))
                        m = tensor.matmul(ps_st2[32:33, :], onescol, sq_b[:, kc, :],
                                          start=(kc == 0), stop=(kc == KC - 1), tile_position=(0, 32))
                    EV[("stats", t)] = inc(m, "pe")

            @blk.tensor
            def _(tensor, e=e, dq_ids=dq_ids):
                t = e - 2
                if live(t):
                    tensor.wait_ge(sems["dv"], EV[("rows", t)])
                    if t >= 1:
                        tensor.wait_ge(sems["dv"], EV[("xr", t - 1)])
                    rr = rrow_b[t % 3]
                    m = tensor.matmul(ps_bc[:, :], onesrow, rr[0:1, 0, :], start=True, stop=True)
                    EV[("bcast", t)] = inc(m, "pe")

            @blk.vector
            def _(vector, e=e):
                t = e - 2
                if live(t):
                    vector.wait_ge(sems["pe"], EV[("bcast", t)])
                    for kc in range(KC):
                        m = vector.tensor_mul(xr_b[t % 2][:, kc, :], xt_b[t % 3][:, kc, :], ps_bc[:, :])
                    EV[("xr", t)] = inc(m, "dv")

            @blk.tensor
            def _(tensor, e=e):
                t = e - 3
                if live(t):
                    tensor.wait_ge(sems["dv"], EV[("xr", t)])
                    if t >= 1:
                        tensor.wait_ge(sems["dv"], EV[("evac", t - 1)])
                        tensor.wait_ge(sems["sc"], EV[("bapre", t - 1)])
                    rr = rrow_b[t % 3]
                    for mc in range(6):
                        for kc in range(KC):
                            tensor.matmul(ps_qkv[:, mc, :], wqkv_t[:, kc, 128 * mc:128 * mc + 128],
                                          xr_b[t % 2][:, kc, :], start=(kc == 0), stop=False)
                        tensor.matmul(ps_qkv[:, mc, :], wsum_t[0:1, 128 * mc:128 * mc + 128],
                                      rr[0:1, 1, :], start=False, stop=True)
                    for kc in range(KC):
                        tensor.matmul(ps_ba[0:34, :], wba_t[:, kc, :], xr_b[t % 2][:, kc, :],
                                      start=(kc == 0), stop=False)
                    m = tensor.matmul(ps_ba[0:34, :], wsum_t[0:1, 768:802], rr[0:1, 1, :],
                                      start=False, stop=True)
                    EV[("qkvz", t)] = inc(m, "pe")

            @blk.vector
            def _(vector, e=e):
                t = e - 1
                if live(t):
                    vector.wait_ge(sems["pe"], EV[("stats", t)])
                    if t >= 2:
                        vector.wait_ge(sems["pe"], EV[("qkvz", t - 2)])
                    if t >= 3:
                        vector.wait_ge(sems["gp"], EV[("rowsst", t - 3)])
                    rr = rrow_b[t % 3]
                    vector.tensor_scalar_mul(rr[0:1, 1, :], ps_st2[0:1, :], 1.0 / D)
                    vector.tensor_scalar_mul(rr[0:1, 0, :], ps_st2[32:33, :], 1.0 / D)
                    vector.tensor_tensor(tmpr[0:1, :], rr[0:1, 1, :], rr[0:1, 1, :], op=ALU.mult)
                    m = vector.tensor_sub(rr[0:1, 0, :], rr[0:1, 0, :], tmpr[0:1, :])
                    EV[("rowsv", t)] = inc(m, "dv")

            @blk.scalar
            def _(scalar, e=e):
                t = e - 1
                if live(t):
                    scalar.wait_ge(sems["dv"], EV[("rowsv", t)])
                    rr = rrow_b[t % 3]
                    m = scalar.activation(rr[0:1, 0, :], rr[0:1, 0, :], AF.Sqrt, bias=1e-5)
                    EV[("rowsq", t)] = inc(m, "sc")

            @blk.vector
            def _(vector, e=e):
                t = e - 1
                if live(t):
                    vector.wait_ge(sems["sc"], EV[("rowsq", t)])
                    rr = rrow_b[t % 3]
                    vector.reciprocal(rr[0:1, 0, :], rr[0:1, 0, :])
                    vector.tensor_tensor(rr[0:1, 1, :], rr[0:1, 1, :], rr[0:1, 0, :], op=ALU.mult)
                    m = vector.tensor_scalar_mul(rr[0:1, 1, :], rr[0:1, 1, :], -1.0)
                    EV[("rows", t)] = inc(m, "dv")

            @blk.vector
            def _(vector, e=e, dq_ids=dq_ids):
                t = e - 3
                if live(t):
                    vector.wait_ge(sems["pe"], EV[("qkvz", t)])
                    ext = ext_b[t % 2]
                    if t % 64 == 0:
                        vector.memset(ext[:, :, 0:3], 0.0)
                    else:
                        vector.tensor_copy(ext[:, :, 0:3], ext_b[(t - 1) % 2][:, :, NT2:NT2 + 3])
                    for cc in range(4):
                        vector.tensor_copy(ext[:, cc, 3:3 + NT2], ps_qkv[:, cc, :])
                    if t >= 2:
                        vector.wait_ge(sems["gp"], EV[("zst", t - 2)])
                    vector.tensor_copy(zt_b[t % 2][:, 0, :], ps_qkv[:, 4, :])
                    m = vector.tensor_copy(zt_b[t % 2][:, 1, :], ps_qkv[:, 5, :])
                    EV[("evac", t)] = inc(m, "dv")
                    if t >= 3:
                        vector.wait_ge(sems["gp"], EV[("qkvst", t - 3)])
                    cv = csil[t % 3]
                    for cc in range(4):
                        vector.tensor_scalar_mul(cv[:, cc, :], ext[:, cc, 0:NT2], convw_t[:, cc, 0:1])
                        for j in range(1, 4):
                            m = vector.scalar_tensor_tensor(
                                cv[:, cc, :], ext[:, cc, j:j + NT2], convw_t[:, cc, j:j + 1],
                                cv[:, cc, :], op0=ALU.mult, op1=ALU.add)
                    EV[("conv", t)] = inc(m, "dv")


            @blk.scalar
            def _(scalar, e=e):
                t = e - 3
                if live(t):
                    scalar.wait_ge(sems["dv"], EV[("conv", t)])
                    if t >= 2:
                        scalar.wait_ge(sems["pe"], EV[("l2mm", t - 2)])
                    cv = csil[t % 3]
                    scalar.activation(cv.ap().rearrange("p c n -> p (c n)"),
                                      cv.ap().rearrange("p c n -> p (c n)"), AF.Silu)
                    scalar.activation(sql2[t % 2][:, 0, :], cv[:, 0, :], AF.Square)
                    m = scalar.activation(sql2[t % 2][:, 1, :], cv[:, 1, :], AF.Square)
                    EV[("silu", t)] = inc(m, "sc")
                    gb = gbt_b[t % 3]
                    if t >= 3:
                        scalar.wait_ge(sems["gp"], EV[("gbst", t - 3)])
                    scalar.activation(gb[0:2, :], ps_ba[0:2, :], AF.Sigmoid)
                    scalar.activation(gb[32:34, :], ps_ba[32:34, :], AF.Exp, bias=gdt_t[32:34, 0:1])
                    m = scalar.activation(gb[32:34, :], gb[32:34, :], AF.Ln, bias=1.0)
                    EV[("bapre", t)] = inc(m, "sc")

            @blk.vector
            def _(vector, e=e):
                t = e - 3
                if live(t):
                    vector.wait_ge(sems["sc"], EV[("bapre", t)])
                    gb = gbt_b[t % 3]
                    m = vector.tensor_scalar_mul(gb[32:34, :], gb[32:34, :], negea_t[32:34, 0:1])
                    EV[("bapost", t)] = inc(m, "dv")

            for i in dq_ids:
                @blk.tensor
                def _(tensor, i=i):
                    kind, c, os = deq_slices[i]
                    if i == 0:
                        tensor.wait_ge(sems["dv"], DV0)
                    if i >= 1:
                        tensor.wait_ge(sems["dv"], EV[("dqw", i - 1)])
                    tensor.wait_ge(sems["gp"], EV[("dqsc", i)])
                    m = tensor.matmul(ps_dq[:, :], selrow, sgrow[i % 2][0:2, :], start=True, stop=True)
                    EV[("dqbc", i)] = inc(m, "pe")

                @blk.vector
                def _(vector, i=i):
                    kind, c, os = deq_slices[i]
                    vector.wait_ge(sems["ld"], EV[("dqld", i)])
                    vector.wait_ge(sems["pe"], EV[("dqbc", i)])
                    if i >= 1:
                        vector.wait_ge(sems["gp"], EV[("dqst", i - 1)])
                    bb, ww = dqb[i % 2], dqw[i % 2]
                    vector.tensor_copy(dq16[:, :], bb[:, :])
                    for half in range(2):
                        if half == 0:
                            vector.tensor_single_scalar(dqn[:, :], dq16[:, :], 15.0, op=ALU.bitwise_and)
                        else:
                            vector.tensor_single_scalar(dqn[:, :], dq16[:, :], 4.0, op=ALU.logical_shift_right)
                            vector.tensor_single_scalar(dqn[:, :], dqn[:, :], 15.0, op=ALU.bitwise_and)
                        vector.tensor_single_scalar(dqm[:, :], dqn[:, :], 7.0, op=ALU.bitwise_and)
                        vector.tensor_scalar(dqf[:, :], dqn[:, :], 8.0, 12.0,
                                             op0=ALU.bitwise_and, op1=ALU.logical_shift_left)
                        vector.tensor_single_scalar(dqg[:, :], dqm[:, :], 6.0, op=ALU.logical_shift_left)
                        vector.tensor_single_scalar(dqg[:, :], dqg[:, :], 16128.0, op=ALU.add)
                        vector.tensor_tensor(dqg[:, :], dqg[:, :], dqf[:, :], op=ALU.bitwise_or)
                        vector.tensor_scalar(dqf[:, :], dqm[:, :], -16064.0, 16128.0,
                                             op0=ALU.mult, op1=ALU.add)
                        vector.tensor_single_scalar(dqm[:, :], dqm[:, :], 2.0, op=ALU.is_lt)
                        vector.tensor_tensor(dqf[:, :], dqf[:, :], dqm[:, :], op=ALU.mult)
                        vector.tensor_tensor(dqg[:, :], dqg[:, :], dqf[:, :], op=ALU.subtract)
                        m = vector.tensor_tensor(ww[:, half, :], dqg.ap().bitcast(BF16)[:, 0:512],
                                                 ps_dq[:, :], op=ALU.mult)
                    EV[("dqw", i)] = inc(m, "dv")

                @blk.gpsimd
                def _(gpsimd, i=i):
                    kind, c, os = deq_slices[i]
                    gpsimd.wait_ge(sems["dv"], EV[("dqw", i)])
                    ww = dqw[i % 2]
                    for half in range(2):
                        kcg = 2 * c + half
                        if kind == "gu":
                            for j in range(4):
                                mcg = 4 * os + j
                                inc(gpsimd.dma_start(sc_gu[mcg, kcg][:, :],
                                                     ww[:, half, 128 * j:128 * j + 128]), "gp", 16)
                        else:
                            inc(gpsimd.dma_start(sc_dn[os, kcg][:, :], ww[:, half, :]), "gp", 16)
                    EV[("dqst", i)] = cnt("gp")

            @blk.tensor
            def _(tensor, e=e):
                t = e - 4
                if live(t):
                    tensor.wait_ge(sems["sc"], EV[("silu", t)])
                    if t >= 2:
                        tensor.wait_ge(sems["sc"], EV[("l2sq", t - 2)])
                        tensor.wait_ge(sems["dv"], EV[("qksc", t - 2)])
                    tensor.matmul(ps_l2[0:1, 0, :], onescol, sql2[t % 2][:, 0, :], start=True, stop=True)
                    m = tensor.matmul(ps_l2[32:33, 0, :], onescol, sql2[t % 2][:, 1, :], start=True,
                                      stop=True, tile_position=(0, 32))
                    EV[("l2mm", t)] = inc(m, "pe")

            @blk.scalar
            def _(scalar, e=e):
                t = e - 4
                if live(t):
                    scalar.wait_ge(sems["pe"], EV[("l2mm", t)])
                    lr, sl = l2row, (0 if t % 2 == 0 else 2)
                    scalar.activation(lr[0:1, sl, :], ps_l2[0:1, 0, :], AF.Sqrt, bias=1e-6)
                    m = scalar.activation(lr[0:1, sl + 1, :], ps_l2[32:33, 0, :], AF.Sqrt, bias=1e-6)
                    EV[("l2sq", t)] = inc(m, "sc")

            @blk.vector
            def _(vector, e=e):
                t = e - 4
                if live(t):
                    vector.wait_ge(sems["sc"], EV[("l2sq", t)])
                    lr, sl = l2row, (0 if t % 2 == 0 else 2)
                    vector.reciprocal(lr[0:1, sl, :], lr[0:1, sl, :])
                    m = vector.reciprocal(lr[0:1, sl + 1, :], lr[0:1, sl + 1, :])
                    EV[("l2rc", t)] = inc(m, "dv")

            @blk.tensor
            def _(tensor, e=e):
                t = e - 5
                if live(t):
                    tensor.wait_ge(sems["dv"], EV[("l2rc", t)])
                    if t >= 1:
                        tensor.wait_ge(sems["dv"], EV[("qksc", t - 1)])
                    if ("l2sq", t + 1) in EV:
                        tensor.wait_ge(sems["sc"], EV[("l2sq", t + 1)])
                    lr, sl = l2row, (0 if t % 2 == 0 else 2)
                    tensor.matmul(ps_l2[:, 1, :], onesrow, lr[0:1, sl, :], start=True, stop=True)
                    m = tensor.matmul(ps_l2[:, 0, :], onesrow, lr[0:1, sl + 1, :], start=True, stop=True)
                    EV[("l2bc", t)] = inc(m, "pe")

            @blk.vector
            def _(vector, e=e):
                t = e - 5
                if live(t):
                    vector.wait_ge(sems["pe"], EV[("l2bc", t)])
                    cv = csil[t % 3]
                    vector.tensor_mul(cv[:, 0, :], cv[:, 0, :], ps_l2[:, 1, :])
                    m = vector.tensor_mul(cv[:, 1, :], cv[:, 1, :], ps_l2[:, 0, :])
                    EV[("qksc", t)] = inc(m, "dv")

            dq_next = [i for i in (2 * e, 2 * e + 1) if i < NDQ]

            @blk.gpsimd
            def _(gpsimd, e=e, dq_ids=dq_ids, dq_next=dq_next):
                t = e - 3
                if live(t):
                    gpsimd.wait_ge(sems["dv"], EV[("evac", t)])
                    c0 = t * NT2
                    m = gpsimd.dma_start(sc_z.ap().rearrange("c p tt -> p c tt")[:, :, c0:c0 + NT2],
                                         zt_b[t % 2].ap())
                    EV[("zst", t)] = inc(m, "gp", 16)
                    gpsimd.wait_ge(sems["dv"], EV[("rows", t)])
                    rr = rrow_b[t % 3]
                    m = gpsimd.dma_start(sc_rows.ap().rearrange("(o c) tt -> o c tt", o=1)[:, :, c0:c0 + NT2],
                                         rr.ap())
                    EV[("rowsst", t)] = inc(m, "gp", 16)
                    gpsimd.wait_ge(sems["dv"], EV[("bapost", t)])
                    inc(gpsimd.dma_start(sc_gb.ap()[0:2, c0:c0 + NT2], gbt_b[t % 3][0:2, :]), "gp", 16)
                    inc(gpsimd.dma_start(sc_gb.ap()[2:4, c0:c0 + NT2], gbt_b[t % 3][32:34, :]), "gp", 16)
                    EV[("gbst", t)] = cnt("gp")
                t = e - 5
                if live(t):
                    gpsimd.wait_ge(sems["dv"], EV[("qksc", t)])
                    c0 = t * NT2
                    cv = csil[t % 3]
                    inc(gpsimd.dma_start(sc_q.ap()[:, c0:c0 + NT2], cv[:, 0, :]), "gp", 16)
                    inc(gpsimd.dma_start(sc_k.ap()[:, c0:c0 + NT2], cv[:, 1, :]), "gp", 16)
                    inc(gpsimd.dma_start(sc_v.ap().rearrange("c p tt -> p c tt")[:, :, c0:c0 + NT2],
                                         cv.ap()[:, 2:4, :]), "gp", 16)
                    EV[("qkvst", t)] = cnt("gp")
                for i in dq_next:
                    kind, c, os = deq_slices[i]
                    if i >= 2:
                        gpsimd.wait_ge(sems["pe"], EV[("dqbc", i - 2)])
                    st = gus if kind == "gu" else dns
                    m = gpsimd.dma_start(sgrow[i % 2][:, :],
                                         st.ap()[2 * c:2 * c + 2, os * 512:os * 512 + 512])
                    EV[("dqsc", i)] = inc(m, "gp", 16)

        p2es.close()

        p4 = ExitStack()
        CHAINS = [(0, 0), (0, 1), (1, 0), (1, 1)]
        GL = NGRP * CH

        qs = [p4.enter_context(nc.sbuf_tensor(f"p4q{i}", [128, 2, GL], F32)) for i in range(2)]
        ks_ = [p4.enter_context(nc.sbuf_tensor(f"p4k{i}", [128, 2, GL], F32)) for i in range(2)]
        gbs = [p4.enter_context(nc.sbuf_tensor(f"p4gb{i}", [1, 4, 2, GL], F32)) for i in range(2)]
        vs = [p4.enter_context(nc.sbuf_tensor(f"p4v{i}", [128, 4, GL], F32)) for i in range(2)]
        zs = [p4.enter_context(nc.sbuf_tensor(f"p4z{i}", [128, 4, GL], F32)) for i in range(2)]
        Sst = [p4.enter_context(nc.sbuf_tensor(f"p4S{j}", [128, 128], F32)) for j in range(8)]
        oslab = [p4.enter_context(nc.sbuf_tensor(f"p4o{h}", [128, GL], F32)) for h in range(4)]
        silz = p4.enter_context(nc.sbuf_tensor("p4silz", [128, GL], F32))
        xf_t = p4.enter_context(nc.sbuf_tensor("p4xf", [128, GL], F32))
        gnrow = p4.enter_context(nc.sbuf_tensor("p4gnr", [1, GL], F32))
        wk = []
        for hh in range(4):
            wk.append(dict(
                cg=p4.enter_context(nc.sbuf_tensor(f"w{hh}cg", [1, CH], F32)),
                bneg=p4.enter_context(nc.sbuf_tensor(f"w{hh}bn", [1, CH], F32)),
                cgcol=p4.enter_context(nc.sbuf_tensor(f"w{hh}cc", [128, 1], F32)),
                bcol=p4.enter_context(nc.sbuf_tensor(f"w{hh}bc", [128, 1], F32)),
                dmin=p4.enter_context(nc.sbuf_tensor(f"w{hh}dm", [128, CH], F32)),
                es_=p4.enter_context(nc.sbuf_tensor(f"w{hh}es", [128, CH], F32)),
                ei=p4.enter_context(nc.sbuf_tensor(f"w{hh}ei", [128, CH], F32)),
                gbx=p4.enter_context(nc.sbuf_tensor(f"w{hh}gb", [128, CH], F32)),
                gbsx=p4.enter_context(nc.sbuf_tensor(f"w{hh}gs", [128, CH], F32)),
                kg=p4.enter_context(nc.sbuf_tensor(f"w{hh}kg", [128, CH], F32)),
                q2=p4.enter_context(nc.sbuf_tensor(f"w{hh}q2", [128, CH], F32)),
                R=[p4.enter_context(nc.sbuf_tensor(f"w{hh}R{j}", [128, CH], F32)) for j in range(7)],
                L=[p4.enter_context(nc.sbuf_tensor(f"w{hh}L{j}", [128, CH], F32)) for j in range(6)],
                X=[p4.enter_context(nc.sbuf_tensor(f"w{hh}X{j}", [128, 128], F32)) for j in range(2)],
                sqk=p4.enter_context(nc.sbuf_tensor(f"w{hh}sk", [128, CH], F32)),
                kpt=p4.enter_context(nc.sbuf_tensor(f"w{hh}kp", [128, 128], F32)),
                ecol=p4.enter_context(nc.sbuf_tensor(f"w{hh}ec", [128, 1], F32)),
                gccol=p4.enter_context(nc.sbuf_tensor(f"w{hh}gc", [128, 1], F32)),
            ))
        psC = [p4.enter_context(nc.psum_tensor(f"p4ps{hh}", [128, 8, 128], F32)) for hh in range(4)]

        @blk.vector
        def _(vector):
            vector.wait_ge(sems["gp"], cnt("gp"))
            for j in range(8):
                m = vector.memset(Sst[j][:, :], 0.0)
            EV[("p4zero",)] = inc(m, "dv")

        def p4_loads(g):
            @blk.gpsimd
            def _(gpsimd, g=g):
                par = g % 2
                gpsimd.wait_ge(sems["gp"], EV[("qkvst", min(64 + 2 * g + 1, NTILES2 - 1))])
                if g >= 2:
                    for h in range(4):
                        gpsimd.wait_ge(sems["dv"], EV[("p4fin", 4 * (g - 2) + NGRP - 1, h)])
                        gpsimd.wait_ge(sems["gp"], EV[("agst", g - 2, h)])
                for b in range(2):
                    c0 = b * S + g * GL
                    inc(gpsimd.dma_start(qs[par].ap()[:, b, :], sc_q.ap()[:, c0:c0 + GL]), "gp", 16)
                    inc(gpsimd.dma_start(ks_[par].ap()[:, b, :], sc_k.ap()[:, c0:c0 + GL]), "gp", 16)
                    inc(gpsimd.dma_start(gbs[par].ap()[:, :, b, :],
                                         sc_gb.ap().rearrange("(o r) t2 -> o r t2", o=1)
                                         [:, :, c0:c0 + GL]), "gp", 16)
                    for vh in range(2):
                        inc(gpsimd.dma_start(vs[par].ap()[:, 2 * b + vh, :],
                                             sc_v.ap()[vh, :, c0:c0 + GL]), "gp", 16)
                        inc(gpsimd.dma_start(zs[par].ap()[:, 2 * b + vh, :],
                                             sc_z.ap()[vh, :, c0:c0 + GL]), "gp", 16)
                EV[("p4ld", g)] = cnt("gp")

        p4_loads(0)
        p4_loads(1)

        def emit_chunk(c, h):
            b, vh = CHAINS[h]
            g, gi = c // NGRP, c % NGRP
            par = g % 2
            w = wk[h]
            col = gi * CH
            Scur = Sst[2 * h + (c % 2)]
            Snew = Sst[2 * h + ((c + 1) % 2)]
            K_ = ks_[par].ap()[:, b, col:col + CH]
            Q_ = qs[par].ap()[:, b, col:col + CH]
            V_ = vs[par].ap()[:, 2 * b + vh, col:col + CH]
            pC = psC[h]

            @blk.vector
            def _(vector):
                vector.wait_ge(sems["gp"], EV[("p4ld", g)])
                if c > 0:
                    vector.wait_ge(sems["dv"], EV[("p4fin", c - 1, h)])
                elif ("p4zero",) in EV:
                    vector.wait_ge(sems["dv"], EV[("p4zero",)])
                if gi == 0 and g >= 1:
                    vector.wait_ge(sems["gp"], EV[("agst", g - 1, h)])
                vector.tensor_copy(w["cg"][0:1, :], gbs[par].ap()[0:1, 2 + vh, b, col:col + CH])
                for j in range(7):
                    sh = 1 << j
                    vector.tensor_add(w["cg"][0:1, sh:CH], w["cg"][0:1, sh:CH], w["cg"][0:1, 0:CH - sh])
                m = vector.tensor_scalar_mul(w["bneg"][0:1, :],
                                             gbs[par].ap()[0:1, vh, b, col:col + CH], -1.0)
                EV[("p4cg", c, h)] = inc(m, "dv")

            @blk.tensor
            def _(tensor):
                tensor.wait_ge(sems["dv"], EV[("p4cg", c, h)])
                tensor.matmul(pC[:, 0, :], onesrow, w["cg"][0:1, :], start=True, stop=True)
                tensor.matmul(pC[:, 3, :], onesrow, w["bneg"][0:1, :], start=True, stop=True)
                tensor.matmul(pC[:, 6, 0:1], w["cg"][0:1, :], ones1, start=True, stop=True)
                m = tensor.matmul(pC[:, 6, 1:2], w["bneg"][0:1, :], ones1, start=True, stop=True)
                EV[("p4t1", c, h)] = inc(m, "pe")

            @blk.vector
            def _(vector):
                vector.wait_ge(sems["pe"], EV[("p4t1", c, h)])
                vector.tensor_copy(w["cgcol"][:, :], pC[:, 6, 0:1])
                vector.tensor_scalar_mul(w["bcol"][:, :], pC[:, 6, 1:2], -1.0)
                m = vector.tensor_scalar(w["dmin"][:, :], pC[:, 0, :], w["cgcol"][:, 0:1], 0.0,
                                         op0=ALU.subtract, op1=ALU.min)
                EV[("p4v2", c, h)] = inc(m, "dv")

            @blk.scalar
            def _(scalar):
                scalar.wait_ge(sems["dv"], EV[("p4v2", c, h)])
                scalar.activation(w["gbx"][:, :], pC[:, 0, :], AF.Exp)
                scalar.activation(w["gbsx"][:, :], pC[:, 0, :], AF.Exp, bias=LN_LNSC)
                m = scalar.activation(w["gccol"][:, :], pC[:, 0, 127:128], AF.Exp)
                EV[("p4a1", c, h)] = inc(m, "sc")

            @blk.vector
            def _(vector):
                vector.wait_ge(sems["sc"], EV[("p4a1", c, h)])
                vector.tensor_add(w["es_"][:, :], w["dmin"][:, :], maskS)
                vector.tensor_add(w["ei"][:, :], w["dmin"][:, :], maskI)
                vector.tensor_mul(w["kg"][:, :], K_, w["gbx"][:, :])
                vector.tensor_mul(w["q2"][:, :], Q_, w["gbsx"][:, :])
                m = vector.tensor_tensor(w["ecol"][:, :], pC[:, 0, 127:128], w["cgcol"][:, :],
                                         op=ALU.subtract)
                EV[("p4v3", c, h)] = inc(m, "dv")

            @blk.scalar
            def _(scalar):
                scalar.wait_ge(sems["dv"], EV[("p4v3", c, h)])
                scalar.activation(w["es_"][:, :], w["es_"][:, :], AF.Exp)
                scalar.activation(w["ei"][:, :], w["ei"][:, :], AF.Exp)
                m = scalar.activation(w["ecol"][:, :], w["ecol"][:, :], AF.Exp)
                EV[("p4a2", c, h)] = inc(m, "sc")

            @blk.tensor
            def _(tensor):
                tensor.wait_ge(sems["sc"], EV[("p4a2", c, h)])
                tensor.matmul(pC[:, 1, :], K_, K_, start=True, stop=True)
                tensor.matmul(pC[:, 2, :], K_, Q_, start=True, stop=True)
                tensor.matmul(pC[:, 6, :], w["kg"][:, :], Scur[:, :], start=True, stop=True)
                m = tensor.transpose(pC[:, 7, :], V_, ident)
                EV[("p4t2", c, h)] = inc(m, "pe")

            @blk.vector
            def _(vector):
                vector.wait_ge(sems["pe"], EV[("p4t2", c, h)])
                vector.tensor_mul(w["R"][0][:, :], pC[:, 1, :], w["es_"][:, :])
                vector.tensor_mul(w["R"][0][:, :], w["R"][0][:, :], pC[:, 3, :])
                vector.tensor_mul(w["sqk"][:, :], pC[:, 2, :], w["ei"][:, :])
                vector.tensor_copy(w["X"][0][:, :], pC[:, 7, :])
                vector.tensor_tensor(w["X"][0][:, :], w["X"][0][:, :], pC[:, 6, :], op=ALU.subtract)
                m = vector.tensor_scalar_mul(w["X"][0][:, :], w["X"][0][:, :], w["bcol"][:, 0:1])
                EV[("p4v4", c, h)] = inc(m, "dv")

            @blk.tensor
            def _(tensor):
                tensor.wait_ge(sems["dv"], EV[("p4v4", c, h)])
                m = tensor.transpose(pC[:, 4, :], w["R"][0][:, :], ident)
                EV[("p4t3", c, h)] = inc(m, "pe")

            @blk.scalar
            def _(scalar):
                scalar.wait_ge(sems["pe"], EV[("p4t3", c, h)])
                m = scalar.copy(w["L"][0][:, :], pC[:, 4, :])
                EV[("p4a3", c, h)] = inc(m, "sc")

            for j in range(6):
                @blk.tensor
                def _(tensor, j=j):
                    if j == 0:
                        tensor.wait_ge(sems["sc"], EV[("p4a3", c, h)])
                    else:
                        tensor.wait_ge(sems["sc"], EV[("p4pw", c, h, j - 1)])
                    m = tensor.matmul(pC[:, 4 + (j % 2), :], w["L"][j][:, :], w["R"][j][:, :],
                                      start=True, stop=True)
                    if j < 5:
                        m = tensor.matmul(pC[:, 4 + ((j + 1) % 2), :], w["R"][j][:, :], w["L"][j][:, :],
                                          start=True, stop=True)
                    EV[("p4pwm", c, h, j)] = inc(m, "pe")

                @blk.scalar
                def _(scalar, j=j):
                    scalar.wait_ge(sems["pe"], EV[("p4pwm", c, h, j)])
                    m = scalar.copy(w["R"][j + 1][:, :], pC[:, 4 + (j % 2), :])
                    if j < 5:
                        m = scalar.copy(w["L"][j + 1][:, :], pC[:, 4 + ((j + 1) % 2), :])
                    EV[("p4pw", c, h, j)] = inc(m, "sc")

            for j in range(7):
                @blk.tensor
                def _(tensor, j=j):
                    if j == 0:
                        tensor.wait_ge(sems["sc"], EV[("p4pw", c, h, 5)])
                        tensor.wait_ge(sems["dv"], EV[("p4v4", c, h)])
                    else:
                        tensor.wait_ge(sems["dv"], EV[("p4ap", c, h, j - 1)])
                    m = tensor.matmul(pC[:, 4 + (j % 2), :], w["R"][j][:, :], w["X"][j % 2][:, :],
                                      start=True, stop=True)
                    EV[("p4apm", c, h, j)] = inc(m, "pe")

                @blk.vector
                def _(vector, j=j):
                    vector.wait_ge(sems["pe"], EV[("p4apm", c, h, j)])
                    m = vector.tensor_add(w["X"][(j + 1) % 2][:, :], pC[:, 4 + (j % 2), :],
                                          w["X"][j % 2][:, :])
                    EV[("p4ap", c, h, j)] = inc(m, "dv")

            @blk.tensor
            def _(tensor):
                tensor.wait_ge(sems["dv"], EV[("p4ap", c, h, 6)])
                m = tensor.transpose(pC[:, 5, :], K_, ident)
                EV[("p4t4", c, h)] = inc(m, "pe")

            @blk.vector
            def _(vector):
                vector.wait_ge(sems["pe"], EV[("p4t4", c, h)])
                m = vector.tensor_scalar_mul(w["kpt"][:, :], pC[:, 5, :], w["ecol"][:, 0:1])
                EV[("p4v5", c, h)] = inc(m, "dv")

            Xd = w["X"][1]

            @blk.tensor
            def _(tensor):
                tensor.wait_ge(sems["dv"], EV[("p4v5", c, h)])
                tensor.matmul(pC[:, 6, :], Xd[:, :], w["sqk"][:, :], start=True, stop=False)
                tensor.matmul(pC[:, 6, :], Scur[:, :], w["q2"][:, :], start=False, stop=True)
                m = tensor.matmul(pC[:, 7, :], w["kpt"][:, :], Xd[:, :], start=True, stop=True)
                EV[("p4t5", c, h)] = inc(m, "pe")

            @blk.vector
            def _(vector):
                vector.wait_ge(sems["pe"], EV[("p4t5", c, h)])
                vector.scalar_tensor_tensor(Snew[:, :], Scur[:, :], w["gccol"][:, 0:1],
                                            pC[:, 7, :], op0=ALU.mult, op1=ALU.add)
                m = vector.tensor_copy(oslab[h][:, col:col + CH], pC[:, 6, :])
                EV[("p4fin", c, h)] = inc(m, "dv")

        def emit_gating(g, h):
            b, vh = CHAINS[h]
            par = g % 2

            @blk.scalar
            def _(scalar):
                scalar.wait_ge(sems["dv"], EV[("p4fin", 4 * g + NGRP - 1, h)])
                m = scalar.activation(silz[:, :], zs[par].ap()[:, 2 * b + vh, :], AF.Silu)
                EV[("gnsz", g, h)] = inc(m, "sc")

            @blk.vector
            def _(vector):
                vector.wait_ge(sems["sc"], EV[("gnsz", g, h)])
                if (g, h) != (0, 0):
                    ph = h - 1 if h > 0 else 3
                    pg = g if h > 0 else g - 1
                    vector.wait_ge(sems["gp"], EV[("agst", pg, ph)])
                m = vector.tensor_mul(xf_t[:, :], oslab[h][:, :], silz[:, :])
                EV[("gnxf", g, h)] = inc(m, "dv")

            @blk.scalar
            def _(scalar):
                scalar.wait_ge(sems["dv"], EV[("gnxf", g, h)])
                m = scalar.activation(silz[:, :], xf_t[:, :], AF.Square)
                EV[("gnsq", g, h)] = inc(m, "sc")

            pgn = psC[h].ap().rearrange("p c n -> p (c n)")[:, 512:1024]

            @blk.tensor
            def _(tensor):
                tensor.wait_ge(sems["sc"], EV[("gnsq", g, h)])
                m = tensor.matmul(pgn[0:1, :], onescol, silz[:, :], start=True, stop=True)
                EV[("gnmm", g, h)] = inc(m, "pe")

            @blk.scalar
            def _(scalar):
                scalar.wait_ge(sems["pe"], EV[("gnmm", g, h)])
                m = scalar.activation(gnrow[0:1, :], pgn[0:1, :], AF.Sqrt, scale=1.0 / 128.0, bias=1e-6)
                EV[("gnsr", g, h)] = inc(m, "sc")

            @blk.vector
            def _(vector):
                vector.wait_ge(sems["sc"], EV[("gnsr", g, h)])
                m = vector.reciprocal(gnrow[0:1, :], gnrow[0:1, :])
                EV[("gnrc", g, h)] = inc(m, "dv")

            @blk.tensor
            def _(tensor):
                tensor.wait_ge(sems["dv"], EV[("gnrc", g, h)])
                m = tensor.matmul(pgn[:, :], onesrow, gnrow[0:1, :], start=True, stop=True)
                EV[("gnbc", g, h)] = inc(m, "pe")

            @blk.vector
            def _(vector):
                vector.wait_ge(sems["pe"], EV[("gnbc", g, h)])
                vector.tensor_mul(xf_t[:, :], xf_t[:, :], pgn[:, :])
                m = vector.tensor_scalar_mul(xf_t[:, :], xf_t[:, :], normw_t[:, 0:1])
                EV[("gnfin", g, h)] = inc(m, "dv")

            @blk.gpsimd
            def _(gpsimd):
                gpsimd.wait_ge(sems["dv"], EV[("gnfin", g, h)])
                c0 = b * S + g * GL
                m = gpsimd.dma_start(ag_in.ap()[128 * vh:128 * vh + 128, c0:c0 + GL], xf_t[:, :])
                EV[("agst", g, h)] = inc(m, "gp", 16)

        for g in range(NGROUPS):
            for ci in range(NGRP):
                for h in range(4):
                    emit_chunk(4 * g + ci, h)
            for h in range(4):
                emit_gating(g, h)
            if g + 2 < NGROUPS:
                p4_loads(g + 2)

        P4_DV = cnt("dv")
        P4_SC = cnt("sc")
        p4.close()

        @blk.gpsimd
        def _(gpsimd):
            gpsimd.wait_ge(sems["gp"], cnt("gp"))
            if debug_scratch:
                inc(gpsimd.dma_start(dbg_ag[:, :], ag_in[:, :]), "gp", 16)
            m = gpsimd.collective_compute("AllGather", ALU.bypass,
                                          replica_groups=[list(range(8))],
                                          ins=[ag_in[:, :]], outs=[ag_out[:, :]])
            EV[("agdone",)] = inc(m, "ag")

        pidh = {}

        @blk.gpsimd
        def _(gpsimd):
            pidh["base"] = gpsimd.alloc_register("gpbase")
            pidh["rc"] = gpsimd.alloc_register("gprc")
            pidv = gpsimd.partition_id()
            gpsimd.reg_alu(pidh["base"], pidv, TCORE, op=ALU.mult)
            gpsimd.wait_ge(sems["gp"], EV[("rowsst", NTILES2 - 1)])
            gpsimd.reg_alu(pidh["rc"], pidh["base"], 0, op=ALU.add)
            m = gpsimd.dma_start(rows_own.ap(),
                                 bass.AP(sc_rows, pidh["rc"],
                                         [[2 * TALL, 1], [TALL, 2], [1, TCORE]]))
            EV[("rowncp",)] = inc(m, "gp", 16)
            gpsimd.wait_ge(sems["ag"], EV[("agdone",)])
            m = gpsimd.dma_start(ag_own.ap(),
                                 bass.AP(ag_out, pidh["rc"],
                                         [[128 * TALL, KC], [TALL, 128], [1, TCORE]]))
            EV[("agowncp",)] = inc(m, "gp", 16)

        p3 = ExitStack()
        xnt3 = p3.enter_context(nc.sbuf_tensor("xnt3", [128, KC, NT3], F32))
        xto3 = p3.enter_context(nc.sbuf_tensor("xto3", [128, KC, NT3], F32))
        rws3 = p3.enter_context(nc.sbuf_tensor("rws3", [1, 2, NT3], F32))
        wgu3 = [p3.enter_context(nc.sbuf_tensor(f"wgu3_{i}", [128, KC, 128], F32)) for i in range(4)]
        h3 = [p3.enter_context(nc.sbuf_tensor(f"h3_{i}", [128, NT3], F32)) for i in range(3)]
        hld3 = [p3.enter_context(nc.sbuf_tensor(f"hld3_{i}", [128, NT3], F32)) for i in range(2)]
        wdn3 = [p3.enter_context(nc.sbuf_tensor(f"wdn3_{i}", [128, NT3], F32)) for i in range(2)]
        y3 = p3.enter_context(nc.sbuf_tensor("y3", [128, KC, NT3], F32))
        yrow3 = p3.enter_context(nc.sbuf_tensor("yrow3", [1, NT3], F32))
        ysq3 = p3.enter_context(nc.sbuf_tensor("ysq3", [128, NT3], F32))
        ps_gu = p3.enter_context(nc.psum_tensor("ps_gu", [128, 4, NT3], F32))
        ps_r3 = p3.enter_context(nc.psum_tensor("ps_r3", [128, NT3], F32))
        ps_m3 = p3.enter_context(nc.psum_tensor("ps_m3", [128, NT3], F32))
        ps_s3 = p3.enter_context(nc.psum_tensor("ps_s3", [128, NT3], F32))

        for tt in range(NTILES3):
            @blk.gpsimd
            def _(gpsimd, tt=tt):
                if tt == 0:
                    gpsimd.wait_ge(sems["dv"], P4_DV)
                    gpsimd.wait_ge(sems["sc"], P4_SC)
                    gpsimd.wait_ge(sems["gp"], EV[("rowncp",)])
                if tt >= 1:
                    gpsimd.wait_ge(sems["pe"], EV[("p3bc", tt - 1)])
                m = gpsimd.dma_start(rws3[:, :, :],
                                     rows_own.ap()[:, :, tt * NT3:(tt + 1) * NT3])
                EV[("p3rw", tt)] = inc(m, "gp", 16)

            @blk.sync
            def _(sync, tt=tt):
                if tt == 0:
                    sync.wait_ge(sems["dv"], P4_DV)
                    sync.wait_ge(sems["sc"], P4_SC)
                if tt >= 1:
                    sync.wait_ge(sems["pe"], EV[("p3gu", tt - 1, 43)])
                for i in range(4):
                    m = sync.dma_start(xto3[:, 4 * i:4 * i + 4, :],
                                       xTo.ap().rearrange("kc p t2 -> p kc t2")
                                       [:, 4 * i:4 * i + 4, tt * NT3:(tt + 1) * NT3])
                    inc(m, "ld", 16)
                EV[("p3xld", tt)] = cnt("ld")

            @blk.tensor
            def _(tensor, tt=tt):
                tensor.wait_ge(sems["gp"], EV[("p3rw", tt)])
                if tt >= 1:
                    tensor.wait_ge(sems["dv"], EV[("p3xn", tt - 1)])
                tensor.matmul(ps_r3[:, :], onesrow, rws3[0:1, 0, :], start=True, stop=True)
                m = tensor.matmul(ps_m3[:, :], onesrow, rws3[0:1, 1, :], start=True, stop=True)
                EV[("p3bc", tt)] = inc(m, "pe")

            @blk.vector
            def _(vector, tt=tt):
                vector.wait_ge(sems["pe"], EV[("p3bc", tt)])
                vector.wait_ge(sems["ld"], EV[("p3xld", tt)])
                for kc in range(KC):
                    vector.tensor_mul(xnt3[:, kc, :], xto3[:, kc, :], ps_r3[:, :])
                    m = vector.tensor_add(xnt3[:, kc, :], xnt3[:, kc, :], ps_m3[:, :])
                EV[("p3xn", tt)] = inc(m, "dv")

            for j in range(44):
                pj = 44 * tt + j

                @blk.sync
                def _(sync, tt=tt, j=j, pj=pj):
                    if pj == 0:
                        sync.wait_ge(sems["gp"], EV[("dqst", NDQ - 1)])
                    if pj >= 2:
                        pt, pjj = divmod(pj - 2, 44)
                        sync.wait_ge(sems["pe"], EV[("p3gu", pt, pjj)])
                    for half in range(2):
                        mc = j if half == 0 else j + 44
                        m = sync.dma_start(wgu3[(2 * pj + half) % 4].ap(),
                                           sc_gu.ap()[mc].rearrange("kc p mm -> p kc mm"))
                        inc(m, "ld", 16)
                    EV[("p3wld", tt, j)] = cnt("ld")

                @blk.tensor
                def _(tensor, tt=tt, j=j, pj=pj):
                    tensor.wait_ge(sems["ld"], EV[("p3wld", tt, j)])
                    tensor.wait_ge(sems["dv"], EV[("p3xn", tt)])
                    if j >= 2:
                        tensor.wait_ge(sems["dv"], EV[("p3h", tt, j - 2)])
                    elif tt >= 1:
                        tensor.wait_ge(sems["dv"], EV[("p3h", tt - 1, 42 + j)])
                    pA = ps_gu.ap()[:, 2 * (j % 2), :]
                    pB = ps_gu.ap()[:, 2 * (j % 2) + 1, :]
                    for kc in range(KC):
                        tensor.matmul(pA, wgu3[(2 * pj) % 4][:, kc, :], xnt3[:, kc, :],
                                      start=(kc == 0), stop=(kc == KC - 1))
                    for kc in range(KC):
                        m = tensor.matmul(pB, wgu3[(2 * pj + 1) % 4][:, kc, :], xnt3[:, kc, :],
                                          start=(kc == 0), stop=(kc == KC - 1))
                    EV[("p3gu", tt, j)] = inc(m, "pe")

                @blk.scalar
                def _(scalar, tt=tt, j=j):
                    scalar.wait_ge(sems["pe"], EV[("p3gu", tt, j)])
                    if j >= 3:
                        scalar.wait_ge(sems["gp"], EV[("p3hst", tt, j - 3)])
                    elif tt >= 1:
                        scalar.wait_ge(sems["gp"], EV[("p3hst", tt - 1, 41 + j)])
                    m = scalar.activation(h3[j % 3][:, :], ps_gu[:, 2 * (j % 2), :], AF.Silu)
                    EV[("p3si", tt, j)] = inc(m, "sc")

                @blk.vector
                def _(vector, tt=tt, j=j):
                    vector.wait_ge(sems["sc"], EV[("p3si", tt, j)])
                    m = vector.tensor_mul(h3[j % 3][:, :], h3[j % 3][:, :],
                                          ps_gu[:, 2 * (j % 2) + 1, :])
                    EV[("p3h", tt, j)] = inc(m, "dv")

                @blk.gpsimd
                def _(gpsimd, tt=tt, j=j):
                    gpsimd.wait_ge(sems["dv"], EV[("p3h", tt, j)])
                    m = gpsimd.dma_start(sc_h[tt % 2, j], h3[j % 3][:, :])
                    EV[("p3hst", tt, j)] = inc(m, "gp", 16)

            for mg in range(4):
                for kc in range(44):
                    lk = 44 * mg + kc

                    @blk.sync
                    def _(sync, tt=tt, mg=mg, kc=kc, lk=lk):
                        if lk >= 2:
                            pmg, pkc = divmod(lk - 2, 44)
                            sync.wait_ge(sems["pe"], EV[("p3dn", tt, pmg, pkc)])
                        sync.wait_ge(sems["gp"], EV[("p3hst", tt, kc)])
                        m = sync.dma_start(wdn3[lk % 2][:, :], sc_dn[mg, kc][:, :])
                        inc(m, "ld", 16)
                        m = sync.dma_start(hld3[lk % 2][:, :], sc_h[tt % 2, kc][:, :])
                        inc(m, "ld", 16)
                        EV[("p3dld", tt, mg, kc)] = cnt("ld")

                    @blk.tensor
                    def _(tensor, tt=tt, mg=mg, kc=kc, lk=lk):
                        tensor.wait_ge(sems["ld"], EV[("p3dld", tt, mg, kc)])
                        if kc == 0:
                            tensor.wait_ge(sems["dv"], EV[("p3h", tt, 43)])
                            if mg >= 1:
                                tensor.wait_ge(sems["dv"], EV[("p3yev", tt, mg - 1)])
                            elif tt >= 1:
                                tensor.wait_ge(sems["dv"], EV[("p3yev", tt - 1, 3)])
                        for mi in range(4):
                            m = tensor.matmul(ps_gu[:, mi, :], wdn3[lk % 2][:, 128 * mi:128 * mi + 128],
                                              hld3[lk % 2][:, :], start=(kc == 0), stop=(kc == 43))
                        EV[("p3dn", tt, mg, kc)] = inc(m, "pe")

                @blk.vector
                def _(vector, tt=tt, mg=mg):
                    vector.wait_ge(sems["pe"], EV[("p3dn", tt, mg, 43)])
                    for mi in range(4):
                        m = vector.tensor_copy(y3[:, 4 * mg + mi, :], ps_gu[:, mi, :])
                    EV[("p3yev", tt, mg)] = inc(m, "dv")

            for mc2 in range(KC):
                @blk.scalar
                def _(scalar, tt=tt, mc2=mc2):
                    if mc2 == 0:
                        scalar.wait_ge(sems["dv"], EV[("p3yev", tt, 3)])
                    if mc2 >= 1:
                        scalar.wait_ge(sems["pe"], EV[("p3yst", tt, mc2 - 1)])
                    m = scalar.activation(ysq3[:, :], y3[:, mc2, :], AF.Square)
                    EV[("p3ysq2", tt, mc2)] = inc(m, "sc")

                @blk.tensor
                def _(tensor, tt=tt, mc2=mc2):
                    tensor.wait_ge(sems["sc"], EV[("p3ysq2", tt, mc2)])
                    if mc2 == 0 and tt >= 1:
                        tensor.wait_ge(sems["sc"], EV[("p3rms", tt - 1)])
                    m = tensor.matmul(ps_s3[0:1, :], onescol, ysq3[:, :],
                                      start=(mc2 == 0), stop=(mc2 == KC - 1))
                    EV[("p3yst", tt, mc2)] = inc(m, "pe")

            @blk.scalar
            def _(scalar, tt=tt):
                scalar.wait_ge(sems["pe"], EV[("p3yst", tt, KC - 1)])
                m = scalar.activation(yrow3[0:1, :], ps_s3[0:1, :], AF.Sqrt,
                                      scale=1.0 / D, bias=1e-6)
                EV[("p3rms", tt)] = inc(m, "sc")

            @blk.vector
            def _(vector, tt=tt):
                vector.wait_ge(sems["sc"], EV[("p3rms", tt)])
                m = vector.reciprocal(yrow3[0:1, :], yrow3[0:1, :])
                EV[("p3rrc", tt)] = inc(m, "dv")

            @blk.tensor
            def _(tensor, tt=tt):
                tensor.wait_ge(sems["dv"], EV[("p3rrc", tt)])
                m = tensor.matmul(ps_r3[:, :], onesrow, yrow3[0:1, :], start=True, stop=True)
                EV[("p3rbc", tt)] = inc(m, "pe")

            @blk.vector
            def _(vector, tt=tt):
                vector.wait_ge(sems["pe"], EV[("p3rbc", tt)])
                for kc in range(KC):
                    m = vector.tensor_mul(y3[:, kc, :], y3[:, kc, :], ps_r3[:, :])
                EV[("p3yr", tt)] = inc(m, "dv")

            @blk.gpsimd
            def _(gpsimd, tt=tt):
                gpsimd.wait_ge(sems["dv"], EV[("p3yr", tt)])
                m = gpsimd.dma_start(sc_y.ap().rearrange("kc p t2 -> p kc t2")
                                     [:, :, tt * NT3:(tt + 1) * NT3], y3.ap())
                EV[("p3yst2", tt)] = inc(m, "gp", 16)

        P3_DV = cnt("dv")
        P3_SC = cnt("sc")
        p3.close()

        p6 = ExitStack()
        agt6 = p6.enter_context(nc.sbuf_tensor("agt6", [128, KC, NT3], F32))
        xto6 = p6.enter_context(nc.sbuf_tensor("xto6", [128, KC, NT3], F32))
        y6 = p6.enter_context(nc.sbuf_tensor("y6", [128, KC, NT3], F32))
        wo6 = [p6.enter_context(nc.sbuf_tensor(f"wo6_{i}", [128, KC, 128], F32)) for i in range(2)]
        o6 = [p6.enter_context(nc.sbuf_tensor(f"o6_{i}", [128, NT3], F32)) for i in range(2)]
        ps_o6 = p6.enter_context(nc.psum_tensor("ps_o6", [128, 2, NT3], F32))

        for tt in range(NTILES3):
            @blk.gpsimd
            def _(gpsimd, tt=tt):
                if tt == 0:
                    gpsimd.wait_ge(sems["gp"], EV[("agowncp",)])
                    gpsimd.wait_ge(sems["dv"], P3_DV)
                    gpsimd.wait_ge(sems["sc"], P3_SC)
                if tt >= 1:
                    gpsimd.wait_ge(sems["pe"], EV[("p6mm", tt - 1, KC - 1)])
                    gpsimd.wait_ge(sems["dv"], EV[("p6o", tt - 1, KC - 1)])
                for i in range(4):
                    m = gpsimd.dma_start(agt6[:, 4 * i:4 * i + 4, :],
                                         ag_own.ap().rearrange("kc p t2 -> p kc t2")
                                         [:, 4 * i:4 * i + 4, tt * NT3:(tt + 1) * NT3])
                    inc(m, "gp", 16)
                EV[("p6ag", tt)] = cnt("gp")

            @blk.sync
            def _(sync, tt=tt):
                if tt == 0:
                    sync.wait_ge(sems["dv"], P3_DV)
                    sync.wait_ge(sems["sc"], P3_SC)
                    sync.wait_ge(sems["gp"], EV[("p3yst2", NTILES3 - 1)])
                if tt >= 1:
                    sync.wait_ge(sems["dv"], EV[("p6o", tt - 1, KC - 1)])
                for i in range(4):
                    inc(sync.dma_start(xto6[:, 4 * i:4 * i + 4, :],
                                       xTo.ap().rearrange("kc p t2 -> p kc t2")
                                       [:, 4 * i:4 * i + 4, tt * NT3:(tt + 1) * NT3]), "ld", 16)
                for i in range(4):
                    inc(sync.dma_start(y6[:, 4 * i:4 * i + 4, :],
                                       sc_y.ap().rearrange("kc p t2 -> p kc t2")
                                       [:, 4 * i:4 * i + 4, tt * NT3:(tt + 1) * NT3]), "ld", 16)
                EV[("p6xld", tt)] = cnt("ld")

            for mc in range(KC):
                lm = KC * tt + mc

                @blk.sync
                def _(sync, tt=tt, mc=mc, lm=lm):
                    if lm >= 2:
                        pt, pm = divmod(lm - 2, KC)
                        sync.wait_ge(sems["pe"], EV[("p6mm", pt, pm)])
                    m = sync.dma_start(wo6[lm % 2].ap(), woutT.ap()[mc].rearrange("kc p mm -> p kc mm"))
                    inc(m, "ld", 16)
                    EV[("p6wld", tt, mc)] = cnt("ld")

                @blk.tensor
                def _(tensor, tt=tt, mc=mc, lm=lm):
                    tensor.wait_ge(sems["ld"], EV[("p6wld", tt, mc)])
                    tensor.wait_ge(sems["gp"], EV[("p6ag", tt)])
                    if mc >= 2:
                        tensor.wait_ge(sems["dv"], EV[("p6o", tt, mc - 2)])
                    elif tt >= 1 and mc < 2:
                        tensor.wait_ge(sems["dv"], EV[("p6o", tt - 1, KC - 2 + mc)])
                    for kc in range(KC):
                        m = tensor.matmul(ps_o6[:, lm % 2, :], wo6[lm % 2][:, kc, :],
                                          agt6[:, kc, :], start=(kc == 0), stop=(kc == KC - 1))
                    EV[("p6mm", tt, mc)] = inc(m, "pe")

                @blk.vector
                def _(vector, tt=tt, mc=mc, lm=lm):
                    vector.wait_ge(sems["pe"], EV[("p6mm", tt, mc)])
                    vector.wait_ge(sems["ld"], EV[("p6xld", tt)])
                    if lm >= 2:
                        vector.wait_ge(sems["gp"], EV[("p6st", tt if mc >= 2 else tt - 1,
                                                       mc - 2 if mc >= 2 else KC - 2 + mc)])
                    ob = o6[lm % 2]
                    vector.tensor_add(ob[:, :], ps_o6[:, lm % 2, :], y6[:, mc, :])
                    m = vector.tensor_add(ob[:, :], ob[:, :], xto6[:, mc, :])
                    EV[("p6o", tt, mc)] = inc(m, "dv")

                @blk.gpsimd
                def _(gpsimd, tt=tt, mc=mc, lm=lm):
                    gpsimd.wait_ge(sems["dv"], EV[("p6o", tt, mc)])
                    m = gpsimd.dma_start(out_d.ap()[mc, :, tt * NT3:(tt + 1) * NT3], o6[lm % 2][:, :])
                    EV[("p6st", tt, mc)] = inc(m, "gp", 16)

        p6.close()

        @blk.gpsimd
        def _(gpsimd):
            gpsimd.wait_ge(sems["gp"], cnt("gp"))

    return nc


def make_consts():
    c = np.zeros((128, 800), dtype=np.float32)
    c[:, 0:128] = np.eye(128, dtype=np.float32)
    s_idx = np.arange(128)[:, None]
    t_idx = np.arange(128)[None, :]
    c[:, 128:256] = np.where(s_idx < t_idx, 0.0, NEGBIG)
    c[:, 256:384] = np.where(s_idx <= t_idx, LN_LNSC, NEGBIG)
    c[:, 384] = 1.0
    c[0, 512:640] = 1.0
    c[0, 640:768] = (np.arange(128) < 64).astype(np.float32)
    c[1, 640:768] = (np.arange(128) >= 64).astype(np.float32)
    for col, val in [(769, 1e-5), (770, 1e-6), (771, 1.0), (772, LN_LNSC),
                     (773, 1.0 / 128.0), (774, 1.0 / 2048.0)]:
        c[:, col] = val
    return c


def stage_inputs(inputs):
    x = np.asarray(inputs["L0_attn_hidden_states"])
    w_qkvz = np.asarray(inputs["L0_attn_w_qkvz"])
    w_ba = np.asarray(inputs["L0_attn_w_ba"])
    conv_w = np.asarray(inputs["L0_attn_conv_w"])
    dt_bias = np.asarray(inputs["L0_attn_dt_bias"])
    A_log = np.asarray(inputs["L0_attn_A_log"])
    norm_w = np.asarray(inputs["L0_attn_norm_weight"])
    w_out = np.asarray(inputs["L0_attn_w_out"])
    gup = np.asarray(inputs["L0_ffn_gate_up_packed"])
    gusc = np.asarray(inputs["L0_ffn_gate_up_scales"])
    dnp = np.asarray(inputs["L0_ffn_down_packed"])
    dnsc = np.asarray(inputs["L0_ffn_down_scales"])

    xTf = np.ascontiguousarray(x.reshape(TALL, D).T[SIGMA, :])
    gub_b = gup.view(np.uint8).reshape(GU_OUT, 1024)
    gub_t = np.ascontiguousarray(gub_b[PI_GU, :].T)
    gub_tiled = np.ascontiguousarray(
        gub_t.reshape(8, 128, 22, 512).transpose(0, 2, 1, 3))
    gus_p = gusc[:, PI_GU]
    gus_2 = np.ascontiguousarray(gus_p)
    dnb_b = dnp.view(np.uint8).reshape(D, 2816)
    dnb_t = np.ascontiguousarray(dnb_b[SIGMA, :].T)
    dnb_tiled = np.ascontiguousarray(
        dnb_t.reshape(22, 128, 4, 512).transpose(0, 2, 1, 3))
    dns_p = dnsc[:, SIGMA]
    dns_2 = np.ascontiguousarray(dns_p)
    consts = make_consts()
    woutT_t = np.ascontiguousarray(
        w_out[SIGMA, :].T.reshape(KC, 128, KC, 128).transpose(2, 0, 1, 3))

    in_maps = []
    for i in range(8):
        wq = w_qkvz[768 * i:768 * (i + 1), SIGMA].T
        wq_t = np.ascontiguousarray(wq.reshape(KC, 128, 768))
        wb = w_ba[4 * i:4 * i + 4, SIGMA].T
        wb_pad = np.zeros((D, 34), dtype=np.float32)
        wb_pad[:, 0:2] = wb[:, 0:2]
        wb_pad[:, 32:34] = wb[:, 2:4]
        wb_t = np.ascontiguousarray(wb_pad.reshape(KC, 128, 34))
        cw = np.concatenate([conv_w[128 * i:128 * (i + 1)],
                             conv_w[1024 + 128 * i:1024 + 128 * (i + 1)],
                             conv_w[2048 + 256 * i:2048 + 256 * (i + 1)]], axis=0)
        cw_t = np.ascontiguousarray(cw.reshape(4, 128, 4))
        gdt_a = np.zeros((34, 2), dtype=np.float32)
        gdt_a[32:34, 0] = dt_bias[2 * i:2 * i + 2]
        gdt_a[32:34, 1] = A_log[2 * i:2 * i + 2]
        xTo_t = np.ascontiguousarray(
            xTf[:, TCORE * i:TCORE * (i + 1)].reshape(KC, 128, TCORE))
        in_maps.append({
            "xT": xTf, "xTo": xTo_t, "wqkvzT": wq_t, "wbaT": wb_t,
            "convw": cw_t, "gdt": gdt_a, "normw": norm_w.reshape(128, 1).astype(np.float32),
            "woutT": woutT_t, "gub": gub_tiled, "gus": gus_2,
            "dnb": dnb_tiled, "dns": dns_2, "consts": consts,
        })
    return in_maps


def kernel(**inputs):
    in_maps = stage_inputs(inputs)
    nc = build_nc()
    res = run_bass_kernel_spmd(nc, in_maps, core_ids=list(range(8)))
    inv = np.argsort(SIGMA)
    parts = []
    for i in range(8):
        o = np.asarray(res.results[i]["out"]).reshape(D, TCORE)
        parts.append(o)
    full = np.concatenate(parts, axis=1)
    full = full[inv, :]
    return np.ascontiguousarray(full.T).reshape(Bb, S, D)


# version 03
# speedup vs baseline: 1.3412x; 1.3412x over previous
import math
import numpy as np
import ml_dtypes
from contextlib import ExitStack

import concourse.bass as bass
import concourse.mybir as mybir
from concourse.bass_utils import run_bass_kernel_spmd

F32 = mybir.dt.float32
F16 = mybir.dt.float16
BF16 = mybir.dt.bfloat16
U8 = mybir.dt.uint8
U16 = mybir.dt.uint16
AF = mybir.ActivationFunctionType
ALU = mybir.AluOpType

D = 2048
Bb = 2
S = 16384
TALL = Bb * S
TCORE = 4096
KC = 16
IFF = 5632
GU_OUT = 11264
NT2 = 256
NTILES2 = TALL // NT2
CH = 128
NCHUNK = S // CH
NGRP = 4
NGROUPS = NCHUNK // NGRP
LNSC = 128.0 ** -0.5
LN_LNSC = math.log(LNSC)
NEGBIG = -1e30
NT3 = 512
NTILES3 = TCORE // NT3


def _interleave_perm(nfull):
    p = []
    for c in range(nfull // 256):
        b = 256 * c
        p.extend(b + 2 * np.arange(128))
        p.extend(b + 2 * np.arange(128) + 1)
    return np.asarray(p, dtype=np.int64)


SIGMA = _interleave_perm(D)
PI = _interleave_perm(IFF)
PI_GU = np.concatenate([PI, IFF + PI])


def build_nc(debug_scratch=False):
    nc = bass.Bass("TRN2", target_bir_lowering=False)
    SK = "ExternalOutput" if debug_scratch else "Internal"

    xT = nc.dram_tensor("xT", [D, TALL], F32, kind="ExternalInput")
    xTo = nc.dram_tensor("xTo", [KC, 128, TCORE], F32, kind="ExternalInput")
    wqkvzT = nc.dram_tensor("wqkvzT", [KC, 128, 768], F32, kind="ExternalInput")
    wbaT = nc.dram_tensor("wbaT", [KC, 128, 34], F32, kind="ExternalInput")
    convw = nc.dram_tensor("convw", [4, 128, 4], F32, kind="ExternalInput")
    gdt = nc.dram_tensor("gdt", [34, 2], F32, kind="ExternalInput")
    normw = nc.dram_tensor("normw", [128, 1], F32, kind="ExternalInput")
    woutT = nc.dram_tensor("woutT", [KC, KC, 128, 128], F32, kind="ExternalInput")
    gub = nc.dram_tensor("gub", [8, 22, 128, 512], U8, kind="ExternalInput")
    gus = nc.dram_tensor("gus", [16, GU_OUT], F16, kind="ExternalInput")
    dnb = nc.dram_tensor("dnb", [22, 4, 128, 512], U8, kind="ExternalInput")
    dns = nc.dram_tensor("dns", [44, D], F16, kind="ExternalInput")
    consts = nc.dram_tensor("consts", [128, 800], F32, kind="ExternalInput")

    out_d = nc.dram_tensor("out", [KC, 128, TCORE], F32, kind="ExternalOutput")

    sc_gu = nc.dram_tensor("sc_gu", [GU_OUT // 128, KC, 128, 128], F32, kind="Internal")
    sc_dn = nc.dram_tensor("sc_dn", [4, 44, 128, 512], F32, kind="Internal")
    sc_q = nc.dram_tensor("sc_q", [128, TALL], F32, kind=SK)
    sc_k = nc.dram_tensor("sc_k", [128, TALL], F32, kind=SK)
    sc_v = nc.dram_tensor("sc_v", [2, 128, TALL], F32, kind=SK)
    sc_z = nc.dram_tensor("sc_z", [2, 128, TALL], F32, kind=SK)
    sc_gb = nc.dram_tensor("sc_gb", [4, TALL], F32, kind=SK)
    sc_rows = nc.dram_tensor("sc_rows", [2, TALL], F32, kind=SK)
    sc_h = nc.dram_tensor("sc_h", [2, 44, 128, NT3], F32, kind="Internal")
    sc_y = nc.dram_tensor("sc_y", [KC, 128, TCORE], F32, kind=SK)
    ag_in = nc.dram_tensor("ag_in", [256, TALL], F32, kind="Internal")
    if debug_scratch:
        dbg_ag = nc.dram_tensor("dbg_ag", [256, TALL], F32, kind="ExternalOutput")
    ag_out = nc.dram_tensor("ag_out", [D, TALL], F32, kind="Internal")
    ag_own = nc.dram_tensor("ag_own", [KC, 128, TCORE], F32, kind="Internal")
    rows_own = nc.dram_tensor("rows_own", [1, 2, TCORE], F32, kind="Internal")

    es = ExitStack()
    n = {}
    EV = {}

    def cnt(s):
        return n.get(s, 0)

    with es:
        sems = {nm: es.enter_context(nc.semaphore(nm)) for nm in
                ["ld", "pe", "dv", "sc", "gp", "ag"]}

        def inc(instr, s, k=1):
            instr.then_inc(sems[s], k)
            n[s] = n.get(s, 0) + k
            return n[s]

        blk = es.enter_context(nc.Block())

        cons = es.enter_context(nc.sbuf_tensor("cons", [128, 800], F32))
        gdt_t = es.enter_context(nc.sbuf_tensor("gdt_t", [34, 2], F32))
        negea_t = es.enter_context(nc.sbuf_tensor("negea_t", [34, 1], F32))
        normw_t = es.enter_context(nc.sbuf_tensor("normw_t", [128, 1], F32))

        for col, val in [(769, 1e-5), (770, 1e-6), (771, 1.0), (772, LN_LNSC),
                         (773, 1.0 / 128.0), (774, 1.0 / D)]:
            nc.const_aps.register(F32, val, cons.ap()[:, col:col + 1])
        ident = cons.ap()[:, 0:128]
        maskS = cons.ap()[:, 128:256]
        maskI = cons.ap()[:, 256:384]
        onescol = cons.ap()[:, 384:385]
        ones1 = cons.ap()[0:1, 384:385]
        onesrow = cons.ap()[0:1, 512:640]
        selrow = cons.ap()[0:2, 640:768]

        p2es = ExitStack()
        wqkv_t = p2es.enter_context(nc.sbuf_tensor("wqkv_t", [128, KC, 768], F32))
        wba_t = p2es.enter_context(nc.sbuf_tensor("wba_t", [128, KC, 34], F32))
        wsum_t = p2es.enter_context(nc.sbuf_tensor("wsum_t", [1, 802], F32))
        convw_t = p2es.enter_context(nc.sbuf_tensor("convw_t", [128, 4, 4], F32))
        ps_dq = p2es.enter_context(nc.psum_tensor("ps_dq", [128, 512], F32))
        ps_w0 = ps_dq

        @blk.sync
        def _(sync):
            for dst, src in [(cons[:, :], consts[:, :]), (gdt_t[:, :], gdt[:, :]),
                             (normw_t[:, :], normw[:, :])]:
                inc(sync.dma_start(dst, src), "ld", 16)
            inc(sync.dma_start(wqkv_t.ap(), wqkvzT.ap().rearrange("kc p m -> p kc m")), "ld", 16)
            inc(sync.dma_start(wba_t.ap(), wbaT.ap().rearrange("kc p m -> p kc m")), "ld", 16)
            inc(sync.dma_start(convw_t.ap(), convw.ap().rearrange("c p j -> p c j")), "ld", 16)
        LD0 = cnt("ld")

        @blk.tensor
        def _(tensor):
            tensor.wait_ge(sems["ld"], LD0)
            for kc in range(KC):
                tensor.matmul(ps_w0[0:1, 0:384], onescol, wqkv_t[:, kc, 0:384],
                              start=(kc == 0), stop=(kc == KC - 1))
            for kc in range(KC):
                tensor.matmul(ps_w0[32:33, 0:384], onescol, wqkv_t[:, kc, 384:768],
                              start=(kc == 0), stop=(kc == KC - 1), tile_position=(0, 32))
            for kc in range(KC):
                m = tensor.matmul(ps_w0[64:65, 0:34], onescol, wba_t[:, kc, :],
                                  start=(kc == 0), stop=(kc == KC - 1), tile_position=(0, 64))
            inc(m, "pe")
        PE0 = cnt("pe")

        @blk.scalar
        def _(scalar):
            scalar.wait_ge(sems["ld"], LD0)
            inc(scalar.activation(negea_t[32:34, 0:1], gdt_t[32:34, 1:2], AF.Exp), "sc")
        SC0 = cnt("sc")

        @blk.vector
        def _(vector):
            vector.wait_ge(sems["pe"], PE0)
            vector.tensor_copy(wsum_t[0:1, 0:384], ps_w0[0:1, 0:384])
            vector.tensor_copy(wsum_t[0:1, 384:768], ps_w0[32:33, 0:384])
            vector.tensor_copy(wsum_t[0:1, 768:802], ps_w0[64:65, 0:34])
            vector.wait_ge(sems["sc"], SC0)
            inc(vector.tensor_scalar_mul(negea_t[32:34, 0:1], negea_t[32:34, 0:1], -1.0), "dv")
        DV0 = cnt("dv")

        deq_slices = [("gu", c, os) for c in range(8) for os in range(22)] + \
                     [("dn", c, os) for c in range(22) for os in range(4)]
        NDQ = len(deq_slices)

        dqb = [p2es.enter_context(nc.sbuf_tensor(f"dqb{i}", [128, 512], U8)) for i in range(2)]
        dq16 = p2es.enter_context(nc.sbuf_tensor("dq16", [128, 512], U16))
        dqn = p2es.enter_context(nc.sbuf_tensor("dqn", [128, 512], U16))
        dqm = p2es.enter_context(nc.sbuf_tensor("dqm", [128, 512], U16))
        dqf = p2es.enter_context(nc.sbuf_tensor("dqf", [128, 512], U16))
        dqg = dqn
        dqw = [p2es.enter_context(nc.sbuf_tensor("dqw0", [128, 2, 512], F32))] * 2
        sgrow = [p2es.enter_context(nc.sbuf_tensor(f"sgrow{i}", [2, 512], F32)) for i in range(2)]

        xt_b = [p2es.enter_context(nc.sbuf_tensor(f"xt{i}", [128, KC, NT2], F32)) for i in range(3)]
        sq_b = p2es.enter_context(nc.sbuf_tensor("sqb", [128, KC, NT2], F32))
        xr_b = [p2es.enter_context(nc.sbuf_tensor(f"xr{i}", [128, KC, NT2], F32)) for i in range(2)]
        rrow_b = [p2es.enter_context(nc.sbuf_tensor(f"rrow{i}", [1, 2, NT2], F32)) for i in range(3)]
        ext_b = [p2es.enter_context(nc.sbuf_tensor(f"ext{i}", [128, 4, NT2 + 3], F32)) for i in range(2)]
        csil = [p2es.enter_context(nc.sbuf_tensor(f"csil{i}", [128, 4, NT2], F32)) for i in range(3)]
        zt_b = [p2es.enter_context(nc.sbuf_tensor(f"zt{i}", [128, 2, NT2], F32)) for i in range(2)]
        gbt_b = [p2es.enter_context(nc.sbuf_tensor(f"gbt{i}", [34, NT2], F32)) for i in range(3)]
        sql2 = [p2es.enter_context(nc.sbuf_tensor(f"sql2_{i}", [128, 2, NT2], F32)) for i in range(2)]
        l2row = p2es.enter_context(nc.sbuf_tensor("l2row", [1, 4, NT2], F32))
        tmpr = p2es.enter_context(nc.sbuf_tensor("tmpr", [1, NT2], F32))
        ps_qkv = p2es.enter_context(nc.psum_tensor("ps_qkv", [128, 6, NT2], F32))
        ps_ba = p2es.enter_context(nc.psum_tensor("ps_ba", [128, NT2], F32))
        ps_st2 = p2es.enter_context(nc.psum_tensor("ps_st2", [128, NT2], F32))
        ps_bc = p2es.enter_context(nc.psum_tensor("ps_bc", [128, NT2], F32))
        ps_l2 = p2es.enter_context(nc.psum_tensor("ps_l2", [128, 2, NT2], F32))

        def live(t):
            return 0 <= t < NTILES2

        for e in range(NTILES2 + 6):
            dq_ids = [i for i in (2 * e - 2, 2 * e - 1) if 0 <= i < NDQ]

            @blk.sync
            def _(sync, e=e, dq_ids=dq_ids):
                t = e
                if live(t):
                    if t >= 3:
                        sync.wait_ge(sems["dv"], EV[("xr", t - 3)])
                    for i in range(4):
                        m = sync.dma_start(
                            xt_b[t % 3][:, 4 * i:4 * i + 4, :],
                            xT.ap().rearrange("(kc p) tt -> p kc tt", p=128)
                              [:, 4 * i:4 * i + 4, t * NT2:(t + 1) * NT2])
                        inc(m, "ld", 16)
                    EV[("xld", t)] = cnt("ld")
                for i in dq_ids:
                    kind, c, os = deq_slices[i]
                    if i >= 2:
                        sync.wait_ge(sems["gp"], EV[("dqst", i - 2)])
                    src = gub[c, os] if kind == "gu" else dnb[c, os]
                    inc(sync.dma_start(dqb[i % 2][:, :], src[:, :]), "ld", 16)
                    EV[("dqld", i)] = cnt("ld")

            @blk.scalar
            def _(scalar, e=e):
                t = e - 1
                if live(t):
                    scalar.wait_ge(sems["ld"], EV[("xld", t)])
                    if t >= 1:
                        scalar.wait_ge(sems["pe"], EV[("stats", t - 1)])
                    for kc in range(KC):
                        m = scalar.activation(sq_b[:, kc, :],
                                              xt_b[t % 3][:, kc, :], AF.Square)
                    EV[("sq", t)] = inc(m, "sc")

            @blk.tensor
            def _(tensor, e=e):
                t = e - 1
                if live(t):
                    tensor.wait_ge(sems["ld"], EV[("xld", t)])
                    if t >= 1:
                        tensor.wait_ge(sems["dv"], EV[("rowsv", t - 1)])
                    tensor.wait_ge(sems["sc"], EV[("sq", t)])
                    for kc in range(KC):
                        tensor.matmul(ps_st2[0:1, :], onescol, xt_b[t % 3][:, kc, :],
                                      start=(kc == 0), stop=(kc == KC - 1))
                        m = tensor.matmul(ps_st2[32:33, :], onescol, sq_b[:, kc, :],
                                          start=(kc == 0), stop=(kc == KC - 1), tile_position=(0, 32))
                    EV[("stats", t)] = inc(m, "pe")

            @blk.tensor
            def _(tensor, e=e, dq_ids=dq_ids):
                t = e - 2
                if live(t):
                    tensor.wait_ge(sems["dv"], EV[("rows", t)])
                    if t >= 1:
                        tensor.wait_ge(sems["dv"], EV[("xr", t - 1)])
                    rr = rrow_b[t % 3]
                    m = tensor.matmul(ps_bc[:, :], onesrow, rr[0:1, 0, :], start=True, stop=True)
                    EV[("bcast", t)] = inc(m, "pe")

            @blk.vector
            def _(vector, e=e):
                t = e - 2
                if live(t):
                    vector.wait_ge(sems["pe"], EV[("bcast", t)])
                    for kc in range(KC):
                        m = vector.tensor_mul(xr_b[t % 2][:, kc, :], xt_b[t % 3][:, kc, :], ps_bc[:, :])
                    EV[("xr", t)] = inc(m, "dv")

            @blk.tensor
            def _(tensor, e=e):
                t = e - 3
                if live(t):
                    tensor.wait_ge(sems["dv"], EV[("xr", t)])
                    if t >= 1:
                        tensor.wait_ge(sems["dv"], EV[("evac", t - 1)])
                        tensor.wait_ge(sems["sc"], EV[("bapre", t - 1)])
                    rr = rrow_b[t % 3]
                    for mc in range(6):
                        for kc in range(KC):
                            tensor.matmul(ps_qkv[:, mc, :], wqkv_t[:, kc, 128 * mc:128 * mc + 128],
                                          xr_b[t % 2][:, kc, :], start=(kc == 0), stop=False)
                        tensor.matmul(ps_qkv[:, mc, :], wsum_t[0:1, 128 * mc:128 * mc + 128],
                                      rr[0:1, 1, :], start=False, stop=True)
                    for kc in range(KC):
                        tensor.matmul(ps_ba[0:34, :], wba_t[:, kc, :], xr_b[t % 2][:, kc, :],
                                      start=(kc == 0), stop=False)
                    m = tensor.matmul(ps_ba[0:34, :], wsum_t[0:1, 768:802], rr[0:1, 1, :],
                                      start=False, stop=True)
                    EV[("qkvz", t)] = inc(m, "pe")

            @blk.vector
            def _(vector, e=e):
                t = e - 1
                if live(t):
                    vector.wait_ge(sems["pe"], EV[("stats", t)])
                    if t >= 2:
                        vector.wait_ge(sems["pe"], EV[("qkvz", t - 2)])
                    if t >= 3:
                        vector.wait_ge(sems["gp"], EV[("rowsst", t - 3)])
                    rr = rrow_b[t % 3]
                    vector.tensor_scalar_mul(rr[0:1, 1, :], ps_st2[0:1, :], 1.0 / D)
                    vector.tensor_scalar_mul(rr[0:1, 0, :], ps_st2[32:33, :], 1.0 / D)
                    vector.tensor_tensor(tmpr[0:1, :], rr[0:1, 1, :], rr[0:1, 1, :], op=ALU.mult)
                    m = vector.tensor_sub(rr[0:1, 0, :], rr[0:1, 0, :], tmpr[0:1, :])
                    EV[("rowsv", t)] = inc(m, "dv")

            @blk.scalar
            def _(scalar, e=e):
                t = e - 1
                if live(t):
                    scalar.wait_ge(sems["dv"], EV[("rowsv", t)])
                    rr = rrow_b[t % 3]
                    m = scalar.activation(rr[0:1, 0, :], rr[0:1, 0, :], AF.Sqrt, bias=1e-5)
                    EV[("rowsq", t)] = inc(m, "sc")

            @blk.vector
            def _(vector, e=e):
                t = e - 1
                if live(t):
                    vector.wait_ge(sems["sc"], EV[("rowsq", t)])
                    rr = rrow_b[t % 3]
                    vector.reciprocal(rr[0:1, 0, :], rr[0:1, 0, :])
                    vector.tensor_tensor(rr[0:1, 1, :], rr[0:1, 1, :], rr[0:1, 0, :], op=ALU.mult)
                    m = vector.tensor_scalar_mul(rr[0:1, 1, :], rr[0:1, 1, :], -1.0)
                    EV[("rows", t)] = inc(m, "dv")

            @blk.vector
            def _(vector, e=e, dq_ids=dq_ids):
                t = e - 3
                if live(t):
                    vector.wait_ge(sems["pe"], EV[("qkvz", t)])
                    ext = ext_b[t % 2]
                    if t % 64 == 0:
                        vector.memset(ext[:, :, 0:3], 0.0)
                    else:
                        vector.tensor_copy(ext[:, :, 0:3], ext_b[(t - 1) % 2][:, :, NT2:NT2 + 3])
                    for cc in range(4):
                        vector.tensor_copy(ext[:, cc, 3:3 + NT2], ps_qkv[:, cc, :])
                    if t >= 2:
                        vector.wait_ge(sems["gp"], EV[("zst", t - 2)])
                    vector.tensor_copy(zt_b[t % 2][:, 0, :], ps_qkv[:, 4, :])
                    m = vector.tensor_copy(zt_b[t % 2][:, 1, :], ps_qkv[:, 5, :])
                    EV[("evac", t)] = inc(m, "dv")
                    if t >= 3:
                        vector.wait_ge(sems["gp"], EV[("qkvst", t - 3)])
                    cv = csil[t % 3]
                    for cc in range(4):
                        vector.tensor_scalar_mul(cv[:, cc, :], ext[:, cc, 0:NT2], convw_t[:, cc, 0:1])
                        for j in range(1, 4):
                            m = vector.scalar_tensor_tensor(
                                cv[:, cc, :], ext[:, cc, j:j + NT2], convw_t[:, cc, j:j + 1],
                                cv[:, cc, :], op0=ALU.mult, op1=ALU.add)
                    EV[("conv", t)] = inc(m, "dv")


            @blk.scalar
            def _(scalar, e=e):
                t = e - 3
                if live(t):
                    scalar.wait_ge(sems["dv"], EV[("conv", t)])
                    if t >= 2:
                        scalar.wait_ge(sems["pe"], EV[("l2mm", t - 2)])
                    cv = csil[t % 3]
                    scalar.activation(cv.ap().rearrange("p c n -> p (c n)"),
                                      cv.ap().rearrange("p c n -> p (c n)"), AF.Silu)
                    scalar.activation(sql2[t % 2][:, 0, :], cv[:, 0, :], AF.Square)
                    m = scalar.activation(sql2[t % 2][:, 1, :], cv[:, 1, :], AF.Square)
                    EV[("silu", t)] = inc(m, "sc")
                    gb = gbt_b[t % 3]
                    if t >= 3:
                        scalar.wait_ge(sems["gp"], EV[("gbst", t - 3)])
                    scalar.activation(gb[0:2, :], ps_ba[0:2, :], AF.Sigmoid)
                    scalar.activation(gb[32:34, :], ps_ba[32:34, :], AF.Exp, bias=gdt_t[32:34, 0:1])
                    m = scalar.activation(gb[32:34, :], gb[32:34, :], AF.Ln, bias=1.0)
                    EV[("bapre", t)] = inc(m, "sc")

            @blk.vector
            def _(vector, e=e):
                t = e - 3
                if live(t):
                    vector.wait_ge(sems["sc"], EV[("bapre", t)])
                    gb = gbt_b[t % 3]
                    m = vector.tensor_scalar_mul(gb[32:34, :], gb[32:34, :], negea_t[32:34, 0:1])
                    EV[("bapost", t)] = inc(m, "dv")

            for i in dq_ids:
                @blk.tensor
                def _(tensor, i=i):
                    kind, c, os = deq_slices[i]
                    if i == 0:
                        tensor.wait_ge(sems["dv"], DV0)
                    if i >= 1:
                        tensor.wait_ge(sems["dv"], EV[("dqw", i - 1)])
                    tensor.wait_ge(sems["gp"], EV[("dqsc", i)])
                    m = tensor.matmul(ps_dq[:, :], selrow, sgrow[i % 2][0:2, :], start=True, stop=True)
                    EV[("dqbc", i)] = inc(m, "pe")

                @blk.vector
                def _(vector, i=i):
                    kind, c, os = deq_slices[i]
                    vector.wait_ge(sems["ld"], EV[("dqld", i)])
                    vector.wait_ge(sems["pe"], EV[("dqbc", i)])
                    if i >= 1:
                        vector.wait_ge(sems["gp"], EV[("dqst", i - 1)])
                    bb, ww = dqb[i % 2], dqw[i % 2]
                    vector.tensor_copy(dq16[:, :], bb[:, :])
                    for half in range(2):
                        if half == 0:
                            vector.tensor_single_scalar(dqn[:, :], dq16[:, :], 15.0, op=ALU.bitwise_and)
                        else:
                            vector.tensor_single_scalar(dqn[:, :], dq16[:, :], 4.0, op=ALU.logical_shift_right)
                            vector.tensor_single_scalar(dqn[:, :], dqn[:, :], 15.0, op=ALU.bitwise_and)
                        vector.tensor_single_scalar(dqm[:, :], dqn[:, :], 7.0, op=ALU.bitwise_and)
                        vector.tensor_scalar(dqf[:, :], dqn[:, :], 8.0, 12.0,
                                             op0=ALU.bitwise_and, op1=ALU.logical_shift_left)
                        vector.tensor_single_scalar(dqg[:, :], dqm[:, :], 6.0, op=ALU.logical_shift_left)
                        vector.tensor_single_scalar(dqg[:, :], dqg[:, :], 16128.0, op=ALU.add)
                        vector.tensor_tensor(dqg[:, :], dqg[:, :], dqf[:, :], op=ALU.bitwise_or)
                        vector.tensor_scalar(dqf[:, :], dqm[:, :], -16064.0, 16128.0,
                                             op0=ALU.mult, op1=ALU.add)
                        vector.tensor_single_scalar(dqm[:, :], dqm[:, :], 2.0, op=ALU.is_lt)
                        vector.tensor_tensor(dqf[:, :], dqf[:, :], dqm[:, :], op=ALU.mult)
                        vector.tensor_tensor(dqg[:, :], dqg[:, :], dqf[:, :], op=ALU.subtract)
                        m = vector.tensor_tensor(ww[:, half, :], dqg.ap().bitcast(BF16)[:, 0:512],
                                                 ps_dq[:, :], op=ALU.mult)
                    EV[("dqw", i)] = inc(m, "dv")

                @blk.gpsimd
                def _(gpsimd, i=i):
                    kind, c, os = deq_slices[i]
                    gpsimd.wait_ge(sems["dv"], EV[("dqw", i)])
                    ww = dqw[i % 2]
                    for half in range(2):
                        kcg = 2 * c + half
                        if kind == "gu":
                            for j in range(4):
                                mcg = 4 * os + j
                                inc(gpsimd.dma_start(sc_gu[mcg, kcg][:, :],
                                                     ww[:, half, 128 * j:128 * j + 128]), "gp", 16)
                        else:
                            inc(gpsimd.dma_start(sc_dn[os, kcg][:, :], ww[:, half, :]), "gp", 16)
                    EV[("dqst", i)] = cnt("gp")

            @blk.tensor
            def _(tensor, e=e):
                t = e - 4
                if live(t):
                    tensor.wait_ge(sems["sc"], EV[("silu", t)])
                    if t >= 2:
                        tensor.wait_ge(sems["sc"], EV[("l2sq", t - 2)])
                        tensor.wait_ge(sems["dv"], EV[("qksc", t - 2)])
                    tensor.matmul(ps_l2[0:1, 0, :], onescol, sql2[t % 2][:, 0, :], start=True, stop=True)
                    m = tensor.matmul(ps_l2[32:33, 0, :], onescol, sql2[t % 2][:, 1, :], start=True,
                                      stop=True, tile_position=(0, 32))
                    EV[("l2mm", t)] = inc(m, "pe")

            @blk.scalar
            def _(scalar, e=e):
                t = e - 4
                if live(t):
                    scalar.wait_ge(sems["pe"], EV[("l2mm", t)])
                    lr, sl = l2row, (0 if t % 2 == 0 else 2)
                    scalar.activation(lr[0:1, sl, :], ps_l2[0:1, 0, :], AF.Sqrt, bias=1e-6)
                    m = scalar.activation(lr[0:1, sl + 1, :], ps_l2[32:33, 0, :], AF.Sqrt, bias=1e-6)
                    EV[("l2sq", t)] = inc(m, "sc")

            @blk.vector
            def _(vector, e=e):
                t = e - 4
                if live(t):
                    vector.wait_ge(sems["sc"], EV[("l2sq", t)])
                    lr, sl = l2row, (0 if t % 2 == 0 else 2)
                    vector.reciprocal(lr[0:1, sl, :], lr[0:1, sl, :])
                    m = vector.reciprocal(lr[0:1, sl + 1, :], lr[0:1, sl + 1, :])
                    EV[("l2rc", t)] = inc(m, "dv")

            @blk.tensor
            def _(tensor, e=e):
                t = e - 5
                if live(t):
                    tensor.wait_ge(sems["dv"], EV[("l2rc", t)])
                    if t >= 1:
                        tensor.wait_ge(sems["dv"], EV[("qksc", t - 1)])
                    if ("l2sq", t + 1) in EV:
                        tensor.wait_ge(sems["sc"], EV[("l2sq", t + 1)])
                    lr, sl = l2row, (0 if t % 2 == 0 else 2)
                    tensor.matmul(ps_l2[:, 1, :], onesrow, lr[0:1, sl, :], start=True, stop=True)
                    m = tensor.matmul(ps_l2[:, 0, :], onesrow, lr[0:1, sl + 1, :], start=True, stop=True)
                    EV[("l2bc", t)] = inc(m, "pe")

            @blk.vector
            def _(vector, e=e):
                t = e - 5
                if live(t):
                    vector.wait_ge(sems["pe"], EV[("l2bc", t)])
                    cv = csil[t % 3]
                    vector.tensor_mul(cv[:, 0, :], cv[:, 0, :], ps_l2[:, 1, :])
                    m = vector.tensor_mul(cv[:, 1, :], cv[:, 1, :], ps_l2[:, 0, :])
                    EV[("qksc", t)] = inc(m, "dv")

            dq_next = [i for i in (2 * e, 2 * e + 1) if i < NDQ]

            @blk.gpsimd
            def _(gpsimd, e=e, dq_ids=dq_ids, dq_next=dq_next):
                t = e - 3
                if live(t):
                    gpsimd.wait_ge(sems["dv"], EV[("evac", t)])
                    c0 = t * NT2
                    m = gpsimd.dma_start(sc_z.ap().rearrange("c p tt -> p c tt")[:, :, c0:c0 + NT2],
                                         zt_b[t % 2].ap())
                    EV[("zst", t)] = inc(m, "gp", 16)
                    gpsimd.wait_ge(sems["dv"], EV[("rows", t)])
                    rr = rrow_b[t % 3]
                    m = gpsimd.dma_start(sc_rows.ap().rearrange("(o c) tt -> o c tt", o=1)[:, :, c0:c0 + NT2],
                                         rr.ap())
                    EV[("rowsst", t)] = inc(m, "gp", 16)
                    gpsimd.wait_ge(sems["dv"], EV[("bapost", t)])
                    inc(gpsimd.dma_start(sc_gb.ap()[0:2, c0:c0 + NT2], gbt_b[t % 3][0:2, :]), "gp", 16)
                    inc(gpsimd.dma_start(sc_gb.ap()[2:4, c0:c0 + NT2], gbt_b[t % 3][32:34, :]), "gp", 16)
                    EV[("gbst", t)] = cnt("gp")
                t = e - 5
                if live(t):
                    gpsimd.wait_ge(sems["dv"], EV[("qksc", t)])
                    c0 = t * NT2
                    cv = csil[t % 3]
                    inc(gpsimd.dma_start(sc_q.ap()[:, c0:c0 + NT2], cv[:, 0, :]), "gp", 16)
                    inc(gpsimd.dma_start(sc_k.ap()[:, c0:c0 + NT2], cv[:, 1, :]), "gp", 16)
                    inc(gpsimd.dma_start(sc_v.ap().rearrange("c p tt -> p c tt")[:, :, c0:c0 + NT2],
                                         cv.ap()[:, 2:4, :]), "gp", 16)
                    EV[("qkvst", t)] = cnt("gp")
                for i in dq_next:
                    kind, c, os = deq_slices[i]
                    if i >= 2:
                        gpsimd.wait_ge(sems["pe"], EV[("dqbc", i - 2)])
                    st = gus if kind == "gu" else dns
                    m = gpsimd.dma_start(sgrow[i % 2][:, :],
                                         st.ap()[2 * c:2 * c + 2, os * 512:os * 512 + 512])
                    EV[("dqsc", i)] = inc(m, "gp", 16)

        p2es.close()

        p4 = ExitStack()
        CHAINS = [(0, 0), (0, 1), (1, 0), (1, 1)]
        GL = NGRP * CH

        qs = [p4.enter_context(nc.sbuf_tensor(f"p4q{i}", [128, 2, GL], F32)) for i in range(2)]
        ks_ = [p4.enter_context(nc.sbuf_tensor(f"p4k{i}", [128, 2, GL], F32)) for i in range(2)]
        gbs = [p4.enter_context(nc.sbuf_tensor(f"p4gb{i}", [1, 4, 2, GL], F32)) for i in range(2)]
        vs = [p4.enter_context(nc.sbuf_tensor(f"p4v{i}", [128, 4, GL], F32)) for i in range(2)]
        zs = [p4.enter_context(nc.sbuf_tensor(f"p4z{i}", [128, 4, GL], F32)) for i in range(2)]
        Sst = [p4.enter_context(nc.sbuf_tensor(f"p4S{j}", [128, 128], F32)) for j in range(8)]
        oslab = [p4.enter_context(nc.sbuf_tensor(f"p4o{h}", [128, GL], F32)) for h in range(4)]
        silz = p4.enter_context(nc.sbuf_tensor("p4silz", [128, GL], F32))
        xf_t = p4.enter_context(nc.sbuf_tensor("p4xf", [128, GL], F32))
        gnrow = p4.enter_context(nc.sbuf_tensor("p4gnr", [1, GL], F32))
        wk = []
        for hh in range(4):
            wk.append(dict(
                cg=p4.enter_context(nc.sbuf_tensor(f"w{hh}cg", [1, CH], F32)),
                bneg=p4.enter_context(nc.sbuf_tensor(f"w{hh}bn", [1, CH], F32)),
                cgcol=p4.enter_context(nc.sbuf_tensor(f"w{hh}cc", [128, 1], F32)),
                bcol=p4.enter_context(nc.sbuf_tensor(f"w{hh}bc", [128, 1], F32)),
                dmin=p4.enter_context(nc.sbuf_tensor(f"w{hh}dm", [128, CH], F32)),
                es_=p4.enter_context(nc.sbuf_tensor(f"w{hh}es", [128, CH], F32)),
                ei=p4.enter_context(nc.sbuf_tensor(f"w{hh}ei", [128, CH], F32)),
                gbx=p4.enter_context(nc.sbuf_tensor(f"w{hh}gb", [128, CH], F32)),
                gbsx=p4.enter_context(nc.sbuf_tensor(f"w{hh}gs", [128, CH], F32)),
                kg=p4.enter_context(nc.sbuf_tensor(f"w{hh}kg", [128, CH], F32)),
                q2=p4.enter_context(nc.sbuf_tensor(f"w{hh}q2", [128, CH], F32)),
                R=[p4.enter_context(nc.sbuf_tensor(f"w{hh}R{j}", [128, CH], F32)) for j in range(7)],
                L=[p4.enter_context(nc.sbuf_tensor(f"w{hh}L{j}", [128, CH], F32)) for j in range(6)],
                X=[p4.enter_context(nc.sbuf_tensor(f"w{hh}X{j}", [128, 128], F32)) for j in range(2)],
                sqk=p4.enter_context(nc.sbuf_tensor(f"w{hh}sk", [128, CH], F32)),
                kpt=p4.enter_context(nc.sbuf_tensor(f"w{hh}kp", [128, 128], F32)),
                ecol=p4.enter_context(nc.sbuf_tensor(f"w{hh}ec", [128, 1], F32)),
                gccol=p4.enter_context(nc.sbuf_tensor(f"w{hh}gc", [128, 1], F32)),
            ))
        psC = [p4.enter_context(nc.psum_tensor(f"p4ps{hh}", [128, 8, 128], F32)) for hh in range(4)]

        @blk.vector
        def _(vector):
            vector.wait_ge(sems["gp"], cnt("gp"))
            for j in range(8):
                m = vector.memset(Sst[j][:, :], 0.0)
            EV[("p4zero",)] = inc(m, "dv")

        def p4_loads(g):
            @blk.gpsimd
            def _(gpsimd, g=g):
                par = g % 2
                gpsimd.wait_ge(sems["gp"], EV[("qkvst", min(64 + 2 * g + 1, NTILES2 - 1))])
                if g >= 2:
                    for h in range(4):
                        gpsimd.wait_ge(sems["dv"], EV[("p4fin", 4 * (g - 2) + NGRP - 1, h)])
                        gpsimd.wait_ge(sems["gp"], EV[("agst", g - 2, h)])
                for b in range(2):
                    c0 = b * S + g * GL
                    inc(gpsimd.dma_start(qs[par].ap()[:, b, :], sc_q.ap()[:, c0:c0 + GL]), "gp", 16)
                    inc(gpsimd.dma_start(ks_[par].ap()[:, b, :], sc_k.ap()[:, c0:c0 + GL]), "gp", 16)
                    inc(gpsimd.dma_start(gbs[par].ap()[:, :, b, :],
                                         sc_gb.ap().rearrange("(o r) t2 -> o r t2", o=1)
                                         [:, :, c0:c0 + GL]), "gp", 16)
                    for vh in range(2):
                        inc(gpsimd.dma_start(vs[par].ap()[:, 2 * b + vh, :],
                                             sc_v.ap()[vh, :, c0:c0 + GL]), "gp", 16)
                        inc(gpsimd.dma_start(zs[par].ap()[:, 2 * b + vh, :],
                                             sc_z.ap()[vh, :, c0:c0 + GL]), "gp", 16)
                EV[("p4ld", g)] = cnt("gp")

        p4_loads(0)
        p4_loads(1)

        def emit_chunk(c, h, seg):
            b, vh = CHAINS[h]
            g, gi = c // NGRP, c % NGRP
            par = g % 2
            w = wk[h]
            col = gi * CH
            Scur = Sst[2 * h + (c % 2)]
            Snew = Sst[2 * h + ((c + 1) % 2)]
            K_ = ks_[par].ap()[:, b, col:col + CH]
            Q_ = qs[par].ap()[:, b, col:col + CH]
            V_ = vs[par].ap()[:, 2 * b + vh, col:col + CH]
            pC = psC[h]

            if seg == 0:
                @blk.vector
                def _(vector):
                    vector.wait_ge(sems["gp"], EV[("p4ld", g)])
                    if c > 0:
                        vector.wait_ge(sems["dv"], EV[("p4fin", c - 1, h)])
                    elif ("p4zero",) in EV:
                        vector.wait_ge(sems["dv"], EV[("p4zero",)])
                    if gi == 0 and g >= 1:
                        vector.wait_ge(sems["gp"], EV[("agst", g - 1, h)])
                    vector.tensor_copy(w["cg"][0:1, :], gbs[par].ap()[0:1, 2 + vh, b, col:col + CH])
                    for j in range(7):
                        sh = 1 << j
                        vector.tensor_add(w["cg"][0:1, sh:CH], w["cg"][0:1, sh:CH], w["cg"][0:1, 0:CH - sh])
                    m = vector.tensor_scalar_mul(w["bneg"][0:1, :],
                                                 gbs[par].ap()[0:1, vh, b, col:col + CH], -1.0)
                    EV[("p4cg", c, h)] = inc(m, "dv")
            elif seg == 1:
                @blk.tensor
                def _(tensor):
                    tensor.wait_ge(sems["dv"], EV[("p4cg", c, h)])
                    tensor.matmul(pC[:, 0, :], onesrow, w["cg"][0:1, :], start=True, stop=True)
                    tensor.matmul(pC[:, 3, :], onesrow, w["bneg"][0:1, :], start=True, stop=True)
                    tensor.matmul(pC[:, 6, 0:1], w["cg"][0:1, :], ones1, start=True, stop=True)
                    m = tensor.matmul(pC[:, 6, 1:2], w["bneg"][0:1, :], ones1, start=True, stop=True)
                    EV[("p4t1", c, h)] = inc(m, "pe")
            elif seg == 2:
                @blk.vector
                def _(vector):
                    vector.wait_ge(sems["pe"], EV[("p4t1", c, h)])
                    vector.tensor_copy(w["cgcol"][:, :], pC[:, 6, 0:1])
                    vector.tensor_scalar_mul(w["bcol"][:, :], pC[:, 6, 1:2], -1.0)
                    m = vector.tensor_scalar(w["dmin"][:, :], pC[:, 0, :], w["cgcol"][:, 0:1], 0.0,
                                             op0=ALU.subtract, op1=ALU.min)
                    EV[("p4v2", c, h)] = inc(m, "dv")
            elif seg == 3:
                @blk.scalar
                def _(scalar):
                    scalar.wait_ge(sems["dv"], EV[("p4v2", c, h)])
                    scalar.activation(w["gbx"][:, :], pC[:, 0, :], AF.Exp)
                    scalar.activation(w["gbsx"][:, :], pC[:, 0, :], AF.Exp, bias=LN_LNSC)
                    m = scalar.activation(w["gccol"][:, :], pC[:, 0, 127:128], AF.Exp)
                    EV[("p4a1", c, h)] = inc(m, "sc")
            elif seg == 4:
                @blk.vector
                def _(vector):
                    vector.wait_ge(sems["sc"], EV[("p4a1", c, h)])
                    vector.tensor_add(w["es_"][:, :], w["dmin"][:, :], maskS)
                    vector.tensor_add(w["ei"][:, :], w["dmin"][:, :], maskI)
                    vector.tensor_mul(w["kg"][:, :], K_, w["gbx"][:, :])
                    vector.tensor_mul(w["q2"][:, :], Q_, w["gbsx"][:, :])
                    m = vector.tensor_tensor(w["ecol"][:, :], pC[:, 0, 127:128], w["cgcol"][:, :],
                                             op=ALU.subtract)
                    EV[("p4v3", c, h)] = inc(m, "dv")
            elif seg == 5:
                @blk.scalar
                def _(scalar):
                    scalar.wait_ge(sems["dv"], EV[("p4v3", c, h)])
                    scalar.activation(w["es_"][:, :], w["es_"][:, :], AF.Exp)
                    scalar.activation(w["ei"][:, :], w["ei"][:, :], AF.Exp)
                    m = scalar.activation(w["ecol"][:, :], w["ecol"][:, :], AF.Exp)
                    EV[("p4a2", c, h)] = inc(m, "sc")
            elif seg == 6:
                @blk.tensor
                def _(tensor):
                    tensor.wait_ge(sems["sc"], EV[("p4a2", c, h)])
                    tensor.matmul(pC[:, 1, :], K_, K_, start=True, stop=True)
                    tensor.matmul(pC[:, 2, :], K_, Q_, start=True, stop=True)
                    tensor.matmul(pC[:, 6, :], w["kg"][:, :], Scur[:, :], start=True, stop=True)
                    m = tensor.transpose(pC[:, 7, :], V_, ident)
                    EV[("p4t2", c, h)] = inc(m, "pe")
            elif seg == 7:
                @blk.vector
                def _(vector):
                    vector.wait_ge(sems["pe"], EV[("p4t2", c, h)])
                    vector.tensor_mul(w["R"][0][:, :], pC[:, 1, :], w["es_"][:, :])
                    vector.tensor_mul(w["R"][0][:, :], w["R"][0][:, :], pC[:, 3, :])
                    vector.tensor_mul(w["sqk"][:, :], pC[:, 2, :], w["ei"][:, :])
                    vector.tensor_copy(w["X"][0][:, :], pC[:, 7, :])
                    vector.tensor_tensor(w["X"][0][:, :], w["X"][0][:, :], pC[:, 6, :], op=ALU.subtract)
                    m = vector.tensor_scalar_mul(w["X"][0][:, :], w["X"][0][:, :], w["bcol"][:, 0:1])
                    EV[("p4v4", c, h)] = inc(m, "dv")
            elif seg == 8:
                @blk.tensor
                def _(tensor):
                    tensor.wait_ge(sems["dv"], EV[("p4v4", c, h)])
                    m = tensor.transpose(pC[:, 4, :], w["R"][0][:, :], ident)
                    EV[("p4t3", c, h)] = inc(m, "pe")
            elif seg == 9:
                @blk.scalar
                def _(scalar):
                    scalar.wait_ge(sems["pe"], EV[("p4t3", c, h)])
                    m = scalar.copy(w["L"][0][:, :], pC[:, 4, :])
                    EV[("p4a3", c, h)] = inc(m, "sc")
            elif 10 <= seg < 22:
                jj, kind = divmod(seg - 10, 2)
                if kind == 0:
                    @blk.tensor
                    def _(tensor, j=jj):
                        if j == 0:
                            tensor.wait_ge(sems["sc"], EV[("p4a3", c, h)])
                        else:
                            tensor.wait_ge(sems["sc"], EV[("p4pw", c, h, j - 1)])
                        m = tensor.matmul(pC[:, 4 + (j % 2), :], w["L"][j][:, :], w["R"][j][:, :],
                                          start=True, stop=True)
                        if j < 5:
                            m = tensor.matmul(pC[:, 4 + ((j + 1) % 2), :], w["R"][j][:, :], w["L"][j][:, :],
                                              start=True, stop=True)
                        EV[("p4pwm", c, h, j)] = inc(m, "pe")
                else:
                    @blk.scalar
                    def _(scalar, j=jj):
                        scalar.wait_ge(sems["pe"], EV[("p4pwm", c, h, j)])
                        m = scalar.copy(w["R"][j + 1][:, :], pC[:, 4 + (j % 2), :])
                        if j < 5:
                            m = scalar.copy(w["L"][j + 1][:, :], pC[:, 4 + ((j + 1) % 2), :])
                        EV[("p4pw", c, h, j)] = inc(m, "sc")
            elif 22 <= seg < 36:
                jj, kind = divmod(seg - 22, 2)
                if kind == 0:
                    @blk.tensor
                    def _(tensor, j=jj):
                        if j == 0:
                            tensor.wait_ge(sems["sc"], EV[("p4pw", c, h, 5)])
                            tensor.wait_ge(sems["dv"], EV[("p4v4", c, h)])
                        else:
                            tensor.wait_ge(sems["dv"], EV[("p4ap", c, h, j - 1)])
                        m = tensor.matmul(pC[:, 4 + (j % 2), :], w["R"][j][:, :], w["X"][j % 2][:, :],
                                          start=True, stop=True)
                        EV[("p4apm", c, h, j)] = inc(m, "pe")
                else:
                    @blk.vector
                    def _(vector, j=jj):
                        vector.wait_ge(sems["pe"], EV[("p4apm", c, h, j)])
                        m = vector.tensor_add(w["X"][(j + 1) % 2][:, :], pC[:, 4 + (j % 2), :],
                                              w["X"][j % 2][:, :])
                        EV[("p4ap", c, h, j)] = inc(m, "dv")
            elif seg == 36:
                @blk.tensor
                def _(tensor):
                    tensor.wait_ge(sems["dv"], EV[("p4ap", c, h, 6)])
                    m = tensor.transpose(pC[:, 5, :], K_, ident)
                    EV[("p4t4", c, h)] = inc(m, "pe")
            elif seg == 37:
                @blk.vector
                def _(vector):
                    vector.wait_ge(sems["pe"], EV[("p4t4", c, h)])
                    m = vector.tensor_scalar_mul(w["kpt"][:, :], pC[:, 5, :], w["ecol"][:, 0:1])
                    EV[("p4v5", c, h)] = inc(m, "dv")
            elif seg == 38:
                Xd = w["X"][1]

                @blk.tensor
                def _(tensor):
                    tensor.wait_ge(sems["dv"], EV[("p4v5", c, h)])
                    tensor.matmul(pC[:, 6, :], Xd[:, :], w["sqk"][:, :], start=True, stop=False)
                    tensor.matmul(pC[:, 6, :], Scur[:, :], w["q2"][:, :], start=False, stop=True)
                    m = tensor.matmul(pC[:, 7, :], w["kpt"][:, :], Xd[:, :], start=True, stop=True)
                    EV[("p4t5", c, h)] = inc(m, "pe")
            elif seg == 39:
                Xd = w["X"][1]

                @blk.vector
                def _(vector):
                    vector.wait_ge(sems["pe"], EV[("p4t5", c, h)])
                    vector.scalar_tensor_tensor(Snew[:, :], Scur[:, :], w["gccol"][:, 0:1],
                                                pC[:, 7, :], op0=ALU.mult, op1=ALU.add)
                    m = vector.tensor_copy(oslab[h][:, col:col + CH], pC[:, 6, :])
                    EV[("p4fin", c, h)] = inc(m, "dv")

        NSEG = 40

        def emit_gating(g, h):
            b, vh = CHAINS[h]
            par = g % 2

            @blk.scalar
            def _(scalar):
                scalar.wait_ge(sems["dv"], EV[("p4fin", 4 * g + NGRP - 1, h)])
                m = scalar.activation(silz[:, :], zs[par].ap()[:, 2 * b + vh, :], AF.Silu)
                EV[("gnsz", g, h)] = inc(m, "sc")

            @blk.vector
            def _(vector):
                vector.wait_ge(sems["sc"], EV[("gnsz", g, h)])
                if (g, h) != (0, 0):
                    ph = h - 1 if h > 0 else 3
                    pg = g if h > 0 else g - 1
                    vector.wait_ge(sems["gp"], EV[("agst", pg, ph)])
                m = vector.tensor_mul(xf_t[:, :], oslab[h][:, :], silz[:, :])
                EV[("gnxf", g, h)] = inc(m, "dv")

            @blk.scalar
            def _(scalar):
                scalar.wait_ge(sems["dv"], EV[("gnxf", g, h)])
                m = scalar.activation(silz[:, :], xf_t[:, :], AF.Square)
                EV[("gnsq", g, h)] = inc(m, "sc")

            pgn = psC[h].ap().rearrange("p c n -> p (c n)")[:, 512:1024]

            @blk.tensor
            def _(tensor):
                tensor.wait_ge(sems["sc"], EV[("gnsq", g, h)])
                m = tensor.matmul(pgn[0:1, :], onescol, silz[:, :], start=True, stop=True)
                EV[("gnmm", g, h)] = inc(m, "pe")

            @blk.scalar
            def _(scalar):
                scalar.wait_ge(sems["pe"], EV[("gnmm", g, h)])
                m = scalar.activation(gnrow[0:1, :], pgn[0:1, :], AF.Sqrt, scale=1.0 / 128.0, bias=1e-6)
                EV[("gnsr", g, h)] = inc(m, "sc")

            @blk.vector
            def _(vector):
                vector.wait_ge(sems["sc"], EV[("gnsr", g, h)])
                m = vector.reciprocal(gnrow[0:1, :], gnrow[0:1, :])
                EV[("gnrc", g, h)] = inc(m, "dv")

            @blk.tensor
            def _(tensor):
                tensor.wait_ge(sems["dv"], EV[("gnrc", g, h)])
                m = tensor.matmul(pgn[:, :], onesrow, gnrow[0:1, :], start=True, stop=True)
                EV[("gnbc", g, h)] = inc(m, "pe")

            @blk.vector
            def _(vector):
                vector.wait_ge(sems["pe"], EV[("gnbc", g, h)])
                vector.tensor_mul(xf_t[:, :], xf_t[:, :], pgn[:, :])
                m = vector.tensor_scalar_mul(xf_t[:, :], xf_t[:, :], normw_t[:, 0:1])
                EV[("gnfin", g, h)] = inc(m, "dv")

            @blk.gpsimd
            def _(gpsimd):
                gpsimd.wait_ge(sems["dv"], EV[("gnfin", g, h)])
                c0 = b * S + g * GL
                m = gpsimd.dma_start(ag_in.ap()[128 * vh:128 * vh + 128, c0:c0 + GL], xf_t[:, :])
                EV[("agst", g, h)] = inc(m, "gp", 16)

        for g in range(NGROUPS):
            for ci in range(NGRP):
                for seg in range(NSEG):
                    for h in range(4):
                        emit_chunk(4 * g + ci, h, seg)
            for h in range(4):
                emit_gating(g, h)
            if g + 2 < NGROUPS:
                p4_loads(g + 2)

        P4_DV = cnt("dv")
        P4_SC = cnt("sc")
        p4.close()

        @blk.gpsimd
        def _(gpsimd):
            gpsimd.wait_ge(sems["gp"], cnt("gp"))
            if debug_scratch:
                inc(gpsimd.dma_start(dbg_ag[:, :], ag_in[:, :]), "gp", 16)
            m = gpsimd.collective_compute("AllGather", ALU.bypass,
                                          replica_groups=[list(range(8))],
                                          ins=[ag_in[:, :]], outs=[ag_out[:, :]])
            EV[("agdone",)] = inc(m, "ag")

        pidh = {}

        @blk.gpsimd
        def _(gpsimd):
            pidh["base"] = gpsimd.alloc_register("gpbase")
            pidh["rc"] = gpsimd.alloc_register("gprc")
            pidv = gpsimd.partition_id()
            gpsimd.reg_alu(pidh["base"], pidv, TCORE, op=ALU.mult)
            gpsimd.wait_ge(sems["gp"], EV[("rowsst", NTILES2 - 1)])
            gpsimd.reg_alu(pidh["rc"], pidh["base"], 0, op=ALU.add)
            m = gpsimd.dma_start(rows_own.ap(),
                                 bass.AP(sc_rows, pidh["rc"],
                                         [[2 * TALL, 1], [TALL, 2], [1, TCORE]]))
            EV[("rowncp",)] = inc(m, "gp", 16)
            gpsimd.wait_ge(sems["ag"], EV[("agdone",)])
            m = gpsimd.dma_start(ag_own.ap(),
                                 bass.AP(ag_out, pidh["rc"],
                                         [[128 * TALL, KC], [TALL, 128], [1, TCORE]]))
            EV[("agowncp",)] = inc(m, "gp", 16)

        p3 = ExitStack()
        xnt3 = p3.enter_context(nc.sbuf_tensor("xnt3", [128, KC, NT3], F32))
        xto3 = p3.enter_context(nc.sbuf_tensor("xto3", [128, KC, NT3], F32))
        rws3 = p3.enter_context(nc.sbuf_tensor("rws3", [1, 2, NT3], F32))
        wgu3 = [p3.enter_context(nc.sbuf_tensor(f"wgu3_{i}", [128, KC, 128], F32)) for i in range(4)]
        h3 = [p3.enter_context(nc.sbuf_tensor(f"h3_{i}", [128, NT3], F32)) for i in range(3)]
        hld3 = [p3.enter_context(nc.sbuf_tensor(f"hld3_{i}", [128, NT3], F32)) for i in range(2)]
        wdn3 = [p3.enter_context(nc.sbuf_tensor(f"wdn3_{i}", [128, NT3], F32)) for i in range(2)]
        y3 = p3.enter_context(nc.sbuf_tensor("y3", [128, KC, NT3], F32))
        yrow3 = p3.enter_context(nc.sbuf_tensor("yrow3", [1, NT3], F32))
        ysq3 = p3.enter_context(nc.sbuf_tensor("ysq3", [128, NT3], F32))
        ps_gu = p3.enter_context(nc.psum_tensor("ps_gu", [128, 4, NT3], F32))
        ps_r3 = p3.enter_context(nc.psum_tensor("ps_r3", [128, NT3], F32))
        ps_m3 = p3.enter_context(nc.psum_tensor("ps_m3", [128, NT3], F32))
        ps_s3 = p3.enter_context(nc.psum_tensor("ps_s3", [128, NT3], F32))

        for tt in range(NTILES3):
            @blk.gpsimd
            def _(gpsimd, tt=tt):
                if tt == 0:
                    gpsimd.wait_ge(sems["dv"], P4_DV)
                    gpsimd.wait_ge(sems["sc"], P4_SC)
                    gpsimd.wait_ge(sems["gp"], EV[("rowncp",)])
                if tt >= 1:
                    gpsimd.wait_ge(sems["pe"], EV[("p3bc", tt - 1)])
                m = gpsimd.dma_start(rws3[:, :, :],
                                     rows_own.ap()[:, :, tt * NT3:(tt + 1) * NT3])
                EV[("p3rw", tt)] = inc(m, "gp", 16)

            @blk.sync
            def _(sync, tt=tt):
                if tt == 0:
                    sync.wait_ge(sems["dv"], P4_DV)
                    sync.wait_ge(sems["sc"], P4_SC)
                if tt >= 1:
                    sync.wait_ge(sems["pe"], EV[("p3gu", tt - 1, 43)])
                for i in range(4):
                    m = sync.dma_start(xto3[:, 4 * i:4 * i + 4, :],
                                       xTo.ap().rearrange("kc p t2 -> p kc t2")
                                       [:, 4 * i:4 * i + 4, tt * NT3:(tt + 1) * NT3])
                    inc(m, "ld", 16)
                EV[("p3xld", tt)] = cnt("ld")

            @blk.tensor
            def _(tensor, tt=tt):
                tensor.wait_ge(sems["gp"], EV[("p3rw", tt)])
                if tt >= 1:
                    tensor.wait_ge(sems["dv"], EV[("p3xn", tt - 1)])
                tensor.matmul(ps_r3[:, :], onesrow, rws3[0:1, 0, :], start=True, stop=True)
                m = tensor.matmul(ps_m3[:, :], onesrow, rws3[0:1, 1, :], start=True, stop=True)
                EV[("p3bc", tt)] = inc(m, "pe")

            @blk.vector
            def _(vector, tt=tt):
                vector.wait_ge(sems["pe"], EV[("p3bc", tt)])
                vector.wait_ge(sems["ld"], EV[("p3xld", tt)])
                for kc in range(KC):
                    vector.tensor_mul(xnt3[:, kc, :], xto3[:, kc, :], ps_r3[:, :])
                    m = vector.tensor_add(xnt3[:, kc, :], xnt3[:, kc, :], ps_m3[:, :])
                EV[("p3xn", tt)] = inc(m, "dv")

            for j in range(44):
                pj = 44 * tt + j

                @blk.sync
                def _(sync, tt=tt, j=j, pj=pj):
                    if pj == 0:
                        sync.wait_ge(sems["gp"], EV[("dqst", NDQ - 1)])
                    if pj >= 2:
                        pt, pjj = divmod(pj - 2, 44)
                        sync.wait_ge(sems["pe"], EV[("p3gu", pt, pjj)])
                    for half in range(2):
                        mc = j if half == 0 else j + 44
                        m = sync.dma_start(wgu3[(2 * pj + half) % 4].ap(),
                                           sc_gu.ap()[mc].rearrange("kc p mm -> p kc mm"))
                        inc(m, "ld", 16)
                    EV[("p3wld", tt, j)] = cnt("ld")

                @blk.tensor
                def _(tensor, tt=tt, j=j, pj=pj):
                    tensor.wait_ge(sems["ld"], EV[("p3wld", tt, j)])
                    tensor.wait_ge(sems["dv"], EV[("p3xn", tt)])
                    if j >= 2:
                        tensor.wait_ge(sems["dv"], EV[("p3h", tt, j - 2)])
                    elif tt >= 1:
                        tensor.wait_ge(sems["dv"], EV[("p3h", tt - 1, 42 + j)])
                    pA = ps_gu.ap()[:, 2 * (j % 2), :]
                    pB = ps_gu.ap()[:, 2 * (j % 2) + 1, :]
                    for kc in range(KC):
                        tensor.matmul(pA, wgu3[(2 * pj) % 4][:, kc, :], xnt3[:, kc, :],
                                      start=(kc == 0), stop=(kc == KC - 1))
                    for kc in range(KC):
                        m = tensor.matmul(pB, wgu3[(2 * pj + 1) % 4][:, kc, :], xnt3[:, kc, :],
                                          start=(kc == 0), stop=(kc == KC - 1))
                    EV[("p3gu", tt, j)] = inc(m, "pe")

                @blk.scalar
                def _(scalar, tt=tt, j=j):
                    scalar.wait_ge(sems["pe"], EV[("p3gu", tt, j)])
                    if j >= 3:
                        scalar.wait_ge(sems["gp"], EV[("p3hst", tt, j - 3)])
                    elif tt >= 1:
                        scalar.wait_ge(sems["gp"], EV[("p3hst", tt - 1, 41 + j)])
                    m = scalar.activation(h3[j % 3][:, :], ps_gu[:, 2 * (j % 2), :], AF.Silu)
                    EV[("p3si", tt, j)] = inc(m, "sc")

                @blk.vector
                def _(vector, tt=tt, j=j):
                    vector.wait_ge(sems["sc"], EV[("p3si", tt, j)])
                    m = vector.tensor_mul(h3[j % 3][:, :], h3[j % 3][:, :],
                                          ps_gu[:, 2 * (j % 2) + 1, :])
                    EV[("p3h", tt, j)] = inc(m, "dv")

                @blk.gpsimd
                def _(gpsimd, tt=tt, j=j):
                    gpsimd.wait_ge(sems["dv"], EV[("p3h", tt, j)])
                    m = gpsimd.dma_start(sc_h[tt % 2, j], h3[j % 3][:, :])
                    EV[("p3hst", tt, j)] = inc(m, "gp", 16)

            for mg in range(4):
                for kc in range(44):
                    lk = 44 * mg + kc

                    @blk.sync
                    def _(sync, tt=tt, mg=mg, kc=kc, lk=lk):
                        if lk >= 2:
                            pmg, pkc = divmod(lk - 2, 44)
                            sync.wait_ge(sems["pe"], EV[("p3dn", tt, pmg, pkc)])
                        sync.wait_ge(sems["gp"], EV[("p3hst", tt, kc)])
                        m = sync.dma_start(wdn3[lk % 2][:, :], sc_dn[mg, kc][:, :])
                        inc(m, "ld", 16)
                        m = sync.dma_start(hld3[lk % 2][:, :], sc_h[tt % 2, kc][:, :])
                        inc(m, "ld", 16)
                        EV[("p3dld", tt, mg, kc)] = cnt("ld")

                    @blk.tensor
                    def _(tensor, tt=tt, mg=mg, kc=kc, lk=lk):
                        tensor.wait_ge(sems["ld"], EV[("p3dld", tt, mg, kc)])
                        if kc == 0:
                            tensor.wait_ge(sems["dv"], EV[("p3h", tt, 43)])
                            if mg >= 1:
                                tensor.wait_ge(sems["dv"], EV[("p3yev", tt, mg - 1)])
                            elif tt >= 1:
                                tensor.wait_ge(sems["dv"], EV[("p3yev", tt - 1, 3)])
                        for mi in range(4):
                            m = tensor.matmul(ps_gu[:, mi, :], wdn3[lk % 2][:, 128 * mi:128 * mi + 128],
                                              hld3[lk % 2][:, :], start=(kc == 0), stop=(kc == 43))
                        EV[("p3dn", tt, mg, kc)] = inc(m, "pe")

                @blk.vector
                def _(vector, tt=tt, mg=mg):
                    vector.wait_ge(sems["pe"], EV[("p3dn", tt, mg, 43)])
                    for mi in range(4):
                        m = vector.tensor_copy(y3[:, 4 * mg + mi, :], ps_gu[:, mi, :])
                    EV[("p3yev", tt, mg)] = inc(m, "dv")

            for mc2 in range(KC):
                @blk.scalar
                def _(scalar, tt=tt, mc2=mc2):
                    if mc2 == 0:
                        scalar.wait_ge(sems["dv"], EV[("p3yev", tt, 3)])
                    if mc2 >= 1:
                        scalar.wait_ge(sems["pe"], EV[("p3yst", tt, mc2 - 1)])
                    m = scalar.activation(ysq3[:, :], y3[:, mc2, :], AF.Square)
                    EV[("p3ysq2", tt, mc2)] = inc(m, "sc")

                @blk.tensor
                def _(tensor, tt=tt, mc2=mc2):
                    tensor.wait_ge(sems["sc"], EV[("p3ysq2", tt, mc2)])
                    if mc2 == 0 and tt >= 1:
                        tensor.wait_ge(sems["sc"], EV[("p3rms", tt - 1)])
                    m = tensor.matmul(ps_s3[0:1, :], onescol, ysq3[:, :],
                                      start=(mc2 == 0), stop=(mc2 == KC - 1))
                    EV[("p3yst", tt, mc2)] = inc(m, "pe")

            @blk.scalar
            def _(scalar, tt=tt):
                scalar.wait_ge(sems["pe"], EV[("p3yst", tt, KC - 1)])
                m = scalar.activation(yrow3[0:1, :], ps_s3[0:1, :], AF.Sqrt,
                                      scale=1.0 / D, bias=1e-6)
                EV[("p3rms", tt)] = inc(m, "sc")

            @blk.vector
            def _(vector, tt=tt):
                vector.wait_ge(sems["sc"], EV[("p3rms", tt)])
                m = vector.reciprocal(yrow3[0:1, :], yrow3[0:1, :])
                EV[("p3rrc", tt)] = inc(m, "dv")

            @blk.tensor
            def _(tensor, tt=tt):
                tensor.wait_ge(sems["dv"], EV[("p3rrc", tt)])
                m = tensor.matmul(ps_r3[:, :], onesrow, yrow3[0:1, :], start=True, stop=True)
                EV[("p3rbc", tt)] = inc(m, "pe")

            @blk.vector
            def _(vector, tt=tt):
                vector.wait_ge(sems["pe"], EV[("p3rbc", tt)])
                for kc in range(KC):
                    m = vector.tensor_mul(y3[:, kc, :], y3[:, kc, :], ps_r3[:, :])
                EV[("p3yr", tt)] = inc(m, "dv")

            @blk.gpsimd
            def _(gpsimd, tt=tt):
                gpsimd.wait_ge(sems["dv"], EV[("p3yr", tt)])
                m = gpsimd.dma_start(sc_y.ap().rearrange("kc p t2 -> p kc t2")
                                     [:, :, tt * NT3:(tt + 1) * NT3], y3.ap())
                EV[("p3yst2", tt)] = inc(m, "gp", 16)

        P3_DV = cnt("dv")
        P3_SC = cnt("sc")
        p3.close()

        p6 = ExitStack()
        agt6 = p6.enter_context(nc.sbuf_tensor("agt6", [128, KC, NT3], F32))
        xto6 = p6.enter_context(nc.sbuf_tensor("xto6", [128, KC, NT3], F32))
        y6 = p6.enter_context(nc.sbuf_tensor("y6", [128, KC, NT3], F32))
        wo6 = [p6.enter_context(nc.sbuf_tensor(f"wo6_{i}", [128, KC, 128], F32)) for i in range(2)]
        o6 = [p6.enter_context(nc.sbuf_tensor(f"o6_{i}", [128, NT3], F32)) for i in range(2)]
        ps_o6 = p6.enter_context(nc.psum_tensor("ps_o6", [128, 2, NT3], F32))

        for tt in range(NTILES3):
            @blk.gpsimd
            def _(gpsimd, tt=tt):
                if tt == 0:
                    gpsimd.wait_ge(sems["gp"], EV[("agowncp",)])
                    gpsimd.wait_ge(sems["dv"], P3_DV)
                    gpsimd.wait_ge(sems["sc"], P3_SC)
                if tt >= 1:
                    gpsimd.wait_ge(sems["pe"], EV[("p6mm", tt - 1, KC - 1)])
                    gpsimd.wait_ge(sems["dv"], EV[("p6o", tt - 1, KC - 1)])
                for i in range(4):
                    m = gpsimd.dma_start(agt6[:, 4 * i:4 * i + 4, :],
                                         ag_own.ap().rearrange("kc p t2 -> p kc t2")
                                         [:, 4 * i:4 * i + 4, tt * NT3:(tt + 1) * NT3])
                    inc(m, "gp", 16)
                EV[("p6ag", tt)] = cnt("gp")

            @blk.sync
            def _(sync, tt=tt):
                if tt == 0:
                    sync.wait_ge(sems["dv"], P3_DV)
                    sync.wait_ge(sems["sc"], P3_SC)
                    sync.wait_ge(sems["gp"], EV[("p3yst2", NTILES3 - 1)])
                if tt >= 1:
                    sync.wait_ge(sems["dv"], EV[("p6o", tt - 1, KC - 1)])
                for i in range(4):
                    inc(sync.dma_start(xto6[:, 4 * i:4 * i + 4, :],
                                       xTo.ap().rearrange("kc p t2 -> p kc t2")
                                       [:, 4 * i:4 * i + 4, tt * NT3:(tt + 1) * NT3]), "ld", 16)
                for i in range(4):
                    inc(sync.dma_start(y6[:, 4 * i:4 * i + 4, :],
                                       sc_y.ap().rearrange("kc p t2 -> p kc t2")
                                       [:, 4 * i:4 * i + 4, tt * NT3:(tt + 1) * NT3]), "ld", 16)
                EV[("p6xld", tt)] = cnt("ld")

            for mc in range(KC):
                lm = KC * tt + mc

                @blk.sync
                def _(sync, tt=tt, mc=mc, lm=lm):
                    if lm >= 2:
                        pt, pm = divmod(lm - 2, KC)
                        sync.wait_ge(sems["pe"], EV[("p6mm", pt, pm)])
                    m = sync.dma_start(wo6[lm % 2].ap(), woutT.ap()[mc].rearrange("kc p mm -> p kc mm"))
                    inc(m, "ld", 16)
                    EV[("p6wld", tt, mc)] = cnt("ld")

                @blk.tensor
                def _(tensor, tt=tt, mc=mc, lm=lm):
                    tensor.wait_ge(sems["ld"], EV[("p6wld", tt, mc)])
                    tensor.wait_ge(sems["gp"], EV[("p6ag", tt)])
                    if mc >= 2:
                        tensor.wait_ge(sems["dv"], EV[("p6o", tt, mc - 2)])
                    elif tt >= 1 and mc < 2:
                        tensor.wait_ge(sems["dv"], EV[("p6o", tt - 1, KC - 2 + mc)])
                    for kc in range(KC):
                        m = tensor.matmul(ps_o6[:, lm % 2, :], wo6[lm % 2][:, kc, :],
                                          agt6[:, kc, :], start=(kc == 0), stop=(kc == KC - 1))
                    EV[("p6mm", tt, mc)] = inc(m, "pe")

                @blk.vector
                def _(vector, tt=tt, mc=mc, lm=lm):
                    vector.wait_ge(sems["pe"], EV[("p6mm", tt, mc)])
                    vector.wait_ge(sems["ld"], EV[("p6xld", tt)])
                    if lm >= 2:
                        vector.wait_ge(sems["gp"], EV[("p6st", tt if mc >= 2 else tt - 1,
                                                       mc - 2 if mc >= 2 else KC - 2 + mc)])
                    ob = o6[lm % 2]
                    vector.tensor_add(ob[:, :], ps_o6[:, lm % 2, :], y6[:, mc, :])
                    m = vector.tensor_add(ob[:, :], ob[:, :], xto6[:, mc, :])
                    EV[("p6o", tt, mc)] = inc(m, "dv")

                @blk.gpsimd
                def _(gpsimd, tt=tt, mc=mc, lm=lm):
                    gpsimd.wait_ge(sems["dv"], EV[("p6o", tt, mc)])
                    m = gpsimd.dma_start(out_d.ap()[mc, :, tt * NT3:(tt + 1) * NT3], o6[lm % 2][:, :])
                    EV[("p6st", tt, mc)] = inc(m, "gp", 16)

        p6.close()

        @blk.gpsimd
        def _(gpsimd):
            gpsimd.wait_ge(sems["gp"], cnt("gp"))

    return nc


def make_consts():
    c = np.zeros((128, 800), dtype=np.float32)
    c[:, 0:128] = np.eye(128, dtype=np.float32)
    s_idx = np.arange(128)[:, None]
    t_idx = np.arange(128)[None, :]
    c[:, 128:256] = np.where(s_idx < t_idx, 0.0, NEGBIG)
    c[:, 256:384] = np.where(s_idx <= t_idx, LN_LNSC, NEGBIG)
    c[:, 384] = 1.0
    c[0, 512:640] = 1.0
    c[0, 640:768] = (np.arange(128) < 64).astype(np.float32)
    c[1, 640:768] = (np.arange(128) >= 64).astype(np.float32)
    for col, val in [(769, 1e-5), (770, 1e-6), (771, 1.0), (772, LN_LNSC),
                     (773, 1.0 / 128.0), (774, 1.0 / 2048.0)]:
        c[:, col] = val
    return c


def stage_inputs(inputs):
    x = np.asarray(inputs["L0_attn_hidden_states"])
    w_qkvz = np.asarray(inputs["L0_attn_w_qkvz"])
    w_ba = np.asarray(inputs["L0_attn_w_ba"])
    conv_w = np.asarray(inputs["L0_attn_conv_w"])
    dt_bias = np.asarray(inputs["L0_attn_dt_bias"])
    A_log = np.asarray(inputs["L0_attn_A_log"])
    norm_w = np.asarray(inputs["L0_attn_norm_weight"])
    w_out = np.asarray(inputs["L0_attn_w_out"])
    gup = np.asarray(inputs["L0_ffn_gate_up_packed"])
    gusc = np.asarray(inputs["L0_ffn_gate_up_scales"])
    dnp = np.asarray(inputs["L0_ffn_down_packed"])
    dnsc = np.asarray(inputs["L0_ffn_down_scales"])

    xTf = np.ascontiguousarray(x.reshape(TALL, D).T[SIGMA, :])
    gub_b = gup.view(np.uint8).reshape(GU_OUT, 1024)
    gub_t = np.ascontiguousarray(gub_b[PI_GU, :].T)
    gub_tiled = np.ascontiguousarray(
        gub_t.reshape(8, 128, 22, 512).transpose(0, 2, 1, 3))
    gus_p = gusc[:, PI_GU]
    gus_2 = np.ascontiguousarray(gus_p)
    dnb_b = dnp.view(np.uint8).reshape(D, 2816)
    dnb_t = np.ascontiguousarray(dnb_b[SIGMA, :].T)
    dnb_tiled = np.ascontiguousarray(
        dnb_t.reshape(22, 128, 4, 512).transpose(0, 2, 1, 3))
    dns_p = dnsc[:, SIGMA]
    dns_2 = np.ascontiguousarray(dns_p)
    consts = make_consts()
    woutT_t = np.ascontiguousarray(
        w_out[SIGMA, :].T.reshape(KC, 128, KC, 128).transpose(2, 0, 1, 3))

    in_maps = []
    for i in range(8):
        wq = w_qkvz[768 * i:768 * (i + 1), SIGMA].T
        wq_t = np.ascontiguousarray(wq.reshape(KC, 128, 768))
        wb = w_ba[4 * i:4 * i + 4, SIGMA].T
        wb_pad = np.zeros((D, 34), dtype=np.float32)
        wb_pad[:, 0:2] = wb[:, 0:2]
        wb_pad[:, 32:34] = wb[:, 2:4]
        wb_t = np.ascontiguousarray(wb_pad.reshape(KC, 128, 34))
        cw = np.concatenate([conv_w[128 * i:128 * (i + 1)],
                             conv_w[1024 + 128 * i:1024 + 128 * (i + 1)],
                             conv_w[2048 + 256 * i:2048 + 256 * (i + 1)]], axis=0)
        cw_t = np.ascontiguousarray(cw.reshape(4, 128, 4))
        gdt_a = np.zeros((34, 2), dtype=np.float32)
        gdt_a[32:34, 0] = dt_bias[2 * i:2 * i + 2]
        gdt_a[32:34, 1] = A_log[2 * i:2 * i + 2]
        xTo_t = np.ascontiguousarray(
            xTf[:, TCORE * i:TCORE * (i + 1)].reshape(KC, 128, TCORE))
        in_maps.append({
            "xT": xTf, "xTo": xTo_t, "wqkvzT": wq_t, "wbaT": wb_t,
            "convw": cw_t, "gdt": gdt_a, "normw": norm_w.reshape(128, 1).astype(np.float32),
            "woutT": woutT_t, "gub": gub_tiled, "gus": gus_2,
            "dnb": dnb_tiled, "dns": dns_2, "consts": consts,
        })
    return in_maps


def kernel(**inputs):
    in_maps = stage_inputs(inputs)
    nc = build_nc()
    res = run_bass_kernel_spmd(nc, in_maps, core_ids=list(range(8)))
    inv = np.argsort(SIGMA)
    parts = []
    for i in range(8):
        o = np.asarray(res.results[i]["out"]).reshape(D, TCORE)
        parts.append(o)
    full = np.concatenate(parts, axis=1)
    full = full[inv, :]
    return np.ascontiguousarray(full.T).reshape(Bb, S, D)
